# version 13
# baseline (speedup 1.0000x reference)
import sys
sys.path.insert(0, '/opt/trn_rl_repo')
import numpy as np
import ml_dtypes
from contextlib import ExitStack

import concourse.bass as bass
import concourse.mybir as mybir
from concourse.bass_utils import run_bass_kernel_spmd
from concourse.alu_op_type import AluOpType as OP

F32 = mybir.dt.float32
BF16 = mybir.dt.bfloat16
U8 = mybir.dt.uint8
I32 = mybir.dt.int32
AF = mybir.ActivationFunctionType
AX = mybir.AxisListType

D = 4096
NCORES = 8
TOK = 1024
EXT = 2048
HD = 128
FF = 11008
RLR = 64
DCH = D // 128
FCH = FF // 128
KSPAN = 1152
NKT = 9
MAGIC = 8388608.0
INVSQ = float(1.0 / np.sqrt(128.0))
BF = ml_dtypes.bfloat16
FWINS = [(i * 512, 512) for i in range(21)] + [(10752, 256)]
DEBUG_TAPS = False


class Sem:
    def __init__(self, nc, es, name):
        self.h = es.enter_context(nc.semaphore(name))
        self.n = 0

    def inc(self, instr, k=1):
        instr.then_inc(self.h, k)
        self.n += k
        return self.n


def build_nc():
    nc = bass.Bass("TRN2", target_bir_lowering=False)
    PE, V, A, G, SY = nc.tensor, nc.vector, nc.scalar, nc.gpsimd, nc.sync

    def din(name, shape, dt=F32):
        return nc.dram_tensor(name, shape, dt, kind="ExternalInput").ap()

    x_ext = din("x_ext", [EXT, D])
    xp_own = din("xp_own", [TOK, D])
    wT = {"q": din("wqT", [D, D]), "k": din("wkT", [D, D]),
          "v": din("wvT", [D, D]), "o": din("woT", [D, D])}
    wg_sl = din("wg_sl", [1408, D])
    wu_sl = din("wu_sl", [1408, D])
    wd_sl = din("wd_sl", [512, FF])
    lrA = din("lrA", [D, RLR])
    lrB = din("lrB", [RLR, D])
    scale_rep = din("scale_rep", [128, D])
    cos_d = din("cos_t", [EXT, HD])
    sin_d = din("sin_t", [EXT, HD])
    masks_d = din("masks", [8, 128, KSPAN], BF16)
    ident_d = din("ident", [128, 128], BF16)
    out_d = nc.dram_tensor("out", [TOK, D], F32, kind="ExternalOutput").ap()

    def scr(name, shape, dt=BF16):
        if DEBUG_TAPS:
            return nc.dram_tensor(name, shape, dt, kind="ExternalOutput").ap()
        return nc.dram_tensor(name, shape, dt).ap()

    xn_tm = scr("xn_tm", [EXT, D])
    xq_tm = scr("xq_tm", [TOK, D])
    q_tm = scr("q_tm", [TOK, D])
    k_tm = scr("k_tm", [EXT, D])
    v_tm = scr("v_tm", [EXT, D])
    s_tm = scr("s_tm", [TOK, D])
    pc_gT = nc.dram_tensor("pc_gT", [D, 1408], BF16).ap()
    pc_uT = nc.dram_tensor("pc_uT", [D, 1408], BF16).ap()
    pc_dT = nc.dram_tensor("pc_dT", [FF, 512], BF16).ap()
    wgqT_g = nc.dram_tensor("wgqT_g", [8 * D, 1408], BF16,
                            addr_space="Shared").ap()
    wuqT_g = nc.dram_tensor("wuqT_g", [8 * D, 1408], BF16,
                            addr_space="Shared").ap()
    wdqT_g = nc.dram_tensor("wdqT_g", [8 * FF, 512], BF16,
                            addr_space="Shared").ap()
    hq_tm = scr("hq_tm", [TOK, FF])
    h_dbg = scr("h_dbg", [TOK, FF], F32) if DEBUG_TAPS else None
    xqf_dbg = scr("xqf_dbg", [128, DCH, TOK]) if DEBUG_TAPS else None
    xqf_dbg2 = scr("xqf_dbg2", [128, DCH, TOK]) if DEBUG_TAPS else None

    es_top = ExitStack()
    sS = Sem(nc, es_top, "sS")
    sC = Sem(nc, es_top, "sC")
    sG = Sem(nc, es_top, "sG")
    sPE = Sem(nc, es_top, "sPE")
    sDV = Sem(nc, es_top, "sDV")
    sAC = Sem(nc, es_top, "sAC")
    sPO = Sem(nc, es_top, "sPO")
    sBAR = Sem(nc, es_top, "sBAR")
    sOUT = Sem(nc, es_top, "sOUT")
    sCCG = Sem(nc, es_top, "sCCG")
    sCCU = Sem(nc, es_top, "sCCU")
    sCCD = Sem(nc, es_top, "sCCD")
    ENGS = [PE, V, A, G, SY]

    def barrier(engs=None):
        engs = ENGS if engs is None else engs
        for e in engs:
            e.sem_inc(sBAR.h, 1)
        sBAR.n += len(engs)
        for e in engs:
            e.wait_ge(sBAR.h, sBAR.n)

    def sb(es, name, shape, dt):
        return es.enter_context(nc.sbuf_tensor(name, shape, dt))

    def psum(es, name, shape, dt=F32):
        return es.enter_context(nc.psum_tensor(name, shape, dt))

    def emit_quant(ws, x, outbf, n, sel_tail):
        nq = n - sel_tail
        nb = nq // 16
        qa, qt = ws["qa"], ws["qt"]
        gl1, gl2, vl1, sgn = ws["gl1"], ws["gl2"], ws["vl1"], ws["sgn"]
        bm, rec, sb6, qmu = ws["bm"], ws["rec"], ws["sb6"], ws["qmu"]
        x3 = x[:, 0:nq].rearrange("p (b k) -> p b k", k=16)
        tiny = nq <= 256

        def vop(instr):
            if tiny:
                V.wait_ge(sDV.h, sDV.inc(instr))
            return instr

        def aop(instr):
            if tiny:
                A.wait_ge(sAC.h, sAC.inc(instr))
            return instr

        if ws.get("_prev", 0):
            A.wait_ge(sDV.h, ws["_prev"])
        t_abs = sAC.inc(A.activation(qa[:, 0:nq], x[:, 0:nq], AF.Abs))
        t_sg = sAC.inc(A.sign(sgn[:, 0:nq], x[:, 0:nq]))
        t_bm = sDV.inc(V.tensor_reduce(bm[:, 0:nb], x3, AX.X, OP.max,
                                       apply_absolute_value=True))
        V.wait_ge(sDV.h, t_bm)
        V.wait_ge(sDV.h, sDV.inc(
            V.tensor_scalar_max(bm[:, 0:nb], bm[:, 0:nb], 1e-30)))
        V.reciprocal(rec[:, 0:nb], bm[:, 0:nb])
        V.wait_ge(sDV.h, sDV.inc(
            V.tensor_scalar_mul(sb6[:, 0:nb], bm[:, 0:nb], 1.0 / 6.0)))
        V.wait_ge(sAC.h, t_abs)
        rec_b = rec[:, 0:nb].unsqueeze(2).broadcast_to([128, nb, 16])
        sb6_b = sb6[:, 0:nb].unsqueeze(2).broadcast_to([128, nb, 16])
        t_t = sDV.inc(V.scalar_tensor_tensor(
            qt[:, 0:nq].rearrange("p (b k) -> p b k", k=16),
            qa[:, 0:nq].rearrange("p (b k) -> p b k", k=16),
            6.0, rec_b, OP.mult, OP.mult))
        A.wait_ge(sDV.h, t_t)
        aop(A.activation(gl1[:, 0:nq], qt[:, 0:nq], AF.Copy, bias=MAGIC))
        t_rn = sAC.inc(A.activation(gl1[:, 0:nq], gl1[:, 0:nq], AF.Copy,
                                    bias=-MAGIC))
        if tiny:
            A.wait_ge(sAC.h, t_rn)
        aop(A.activation(gl2[:, 0:nq], qt[:, 0:nq], AF.Copy, bias=MAGIC,
                         scale=2.0))
        t_hu = sAC.inc(A.activation(gl2[:, 0:nq], gl2[:, 0:nq], AF.Copy,
                                    bias=-MAGIC * 0.5, scale=0.5))
        if tiny:
            A.wait_ge(sAC.h, t_hu)
        V.wait_ge(sAC.h, t_rn)
        vop(V.tensor_scalar_min(gl1[:, 0:nq], gl1[:, 0:nq], 4.0))
        vop(V.tensor_scalar(vl1[:, 0:nq], qt[:, 0:nq], 5.0, 2.0,
                            OP.is_ge, OP.mult))
        vop(V.tensor_add(gl1[:, 0:nq], gl1[:, 0:nq], vl1[:, 0:nq]))
        vop(V.tensor_scalar(qmu[:, 0:nq], qt[:, 0:nq], 2.0, None, OP.is_lt))
        V.wait_ge(sAC.h, t_hu)
        vop(V.select(qt[:, 0:nq], qmu[:, 0:nq], gl2[:, 0:nq], gl1[:, 0:nq]))
        V.wait_ge(sAC.h, t_sg)
        vop(V.tensor_mul(qt[:, 0:nq], qt[:, 0:nq], sgn[:, 0:nq]))
        last = V.tensor_tensor(
            outbf[:, 0:nq].rearrange("p (b k) -> p b k", k=16),
            qt[:, 0:nq].rearrange("p (b k) -> p b k", k=16),
            sb6_b, OP.mult)
        if sel_tail:
            last = V.tensor_copy(outbf[:, nq:n], x[:, nq:n])
        ws["_prev"] = sDV.inc(last)
        return ws["_prev"]

    def quant_ws(es, n, tag):
        return {
            "qa": sb(es, "qa" + tag, [128, n], F32),
            "qt": sb(es, "qt" + tag, [128, n], F32),
            "gl1": sb(es, "gl1" + tag, [128, n], F32),
            "gl2": sb(es, "gl2" + tag, [128, n], F32),
            "vl1": sb(es, "vl1" + tag, [128, n], F32),
            "sgn": sb(es, "sgn" + tag, [128, n], F32),
            "qmu": sb(es, "qmu" + tag, [128, n], U8),
            "bm": sb(es, "bm" + tag, [128, n // 16], F32),
            "rec": sb(es, "rec" + tag, [128, n // 16], F32),
            "sb6": sb(es, "sb6" + tag, [128, n // 16], F32),
        }

    t1_bf = sb(es_top, "t1_bf", [64, TOK], BF16)
    ident_sb = sb(es_top, "ident_sb", [128, 128], BF16)
    cvals = [MAGIC, -MAGIC, -MAGIC * 0.5]
    cbuf = sb(es_top, "cbuf", [128, len(cvals)], F32)
    for i, v in enumerate(cvals):
        nc.const_aps.register(F32, v, cbuf[:, i:i + 1])
    xn_store = [0] * 16

    with es_top, nc.Block() as block:

        @block.sync
        def _(_sync):
            for i, v in enumerate(cvals):
                V.memset(cbuf[:, i:i + 1], v)
            t_id = sS.inc(SY.dma_start(out=ident_sb[:], in_=ident_d), 16)
            PE.wait_ge(sS.h, t_id)
            with ExitStack() as es:
                xb = [sb(es, f"xb{i}", [128, D], F32) for i in range(2)]
                xpb = [sb(es, f"xpb{i}", [128, D], F32) for i in range(2)]
                xnb = [sb(es, f"xnb{i}", [128, D], BF16) for i in range(2)]
                xqb = [sb(es, f"xqb{i}", [128, D], BF16) for i in range(2)]
                xnp = sb(es, "xnp", [128, D], F32)
                sqd = sb(es, "sqd", [128, D], F32)
                sqp = sb(es, "sqp", [128, 4], F32)
                st0 = sb(es, "st0", [128, 1], F32)
                st1 = sb(es, "st1", [128, 1], F32)
                stm = sb(es, "stm", [128, 1], F32)
                stv = sb(es, "stv", [128, 1], F32)
                sts = sb(es, "sts", [128, 1], F32)
                str_ = sb(es, "str", [128, 1], F32)
                stb = sb(es, "stb", [128, 1], F32)
                ws = quant_ws(es, 2048, "_ln")

                xb_free = [0, 0]
                xb_sq_free = [0, 0]
                xpb_free = [0, 0]
                xnb_free = [0, 0]
                xqb_free = [0, 0]
                xnp_guard = [0]

                for s in range(16):
                    own = s >= 8
                    b = s % 2
                    if s >= 2:
                        SY.wait_ge(sDV.h, xb_free[b])
                        SY.wait_ge(sAC.h, xb_sq_free[b])
                    t_ld = sS.inc(SY.dma_start(
                        out=xb[b][:], in_=x_ext[s * 128:(s + 1) * 128, :]), 16)
                    t_ldp = 0
                    if own:
                        if s >= 10:
                            SY.wait_ge(sDV.h, xpb_free[b])
                        r0 = (s - 8) * 128
                        t_ldp = sS.inc(SY.dma_start(
                            out=xpb[b][:], in_=xp_own[r0:r0 + 128, :]), 16)
                    V.wait_ge(sS.h, t_ld)
                    t_s0 = sDV.inc(V.reduce_sum(st0[:], xb[b][:], AX.X))
                    A.wait_ge(sS.h, t_ld)
                    t_s1 = 0
                    for q4 in range(4):
                        t_s1 = sAC.inc(A.activation(
                            sqd[:, q4 * 1024:(q4 + 1) * 1024],
                            xb[b][:, q4 * 1024:(q4 + 1) * 1024],
                            AF.Square, accum_out=sqp[:, q4:q4 + 1]))
                    xb_sq_free[b] = t_s1
                    V.wait_ge(sAC.h, t_s1)
                    t_s2 = sDV.inc(V.reduce_sum(st1[:], sqp[:], AX.X))
                    V.wait_ge(sDV.h, t_s2)
                    V.wait_ge(sDV.h, t_s0)

                    def vtiny(instr):
                        V.wait_ge(sDV.h, sDV.inc(instr))

                    vtiny(V.tensor_scalar_mul(stm[:], st0[:], 1.0 / D))
                    vtiny(V.tensor_scalar(stv[:], st1[:], 1.0 / D, 1e-5,
                                          OP.mult, OP.add))
                    vtiny(V.tensor_mul(sts[:], stm[:], stm[:]))
                    t_var = sDV.inc(V.tensor_sub(stv[:], stv[:], sts[:]))
                    A.wait_ge(sDV.h, t_var)
                    t_sqr = sAC.inc(A.activation(sts[:], stv[:], AF.Sqrt))
                    V.wait_ge(sAC.h, t_sqr)
                    vtiny(V.reciprocal(str_[:], sts[:]))
                    t_rs = sDV.inc(V.scalar_tensor_tensor(
                        stb[:], stm[:], -1.0, str_[:], OP.mult, OP.mult))
                    V.wait_ge(sDV.h, t_rs)
                    if s >= 2:
                        V.wait_ge(sS.h, xnb_free[b])
                    t_xn = sDV.inc(V.tensor_scalar(
                        xnb[b][:], xb[b][:], str_[:], stb[:],
                        OP.mult, OP.add))
                    xb_free[b] = t_xn
                    SY.wait_ge(sDV.h, t_xn)
                    xn_store[s] = sS.inc(SY.dma_start(
                        out=xn_tm[s * 128:(s + 1) * 128, :], in_=xnb[b][:]), 16)
                    xnb_free[b] = xn_store[s]
                    if own:
                        V.wait_ge(sS.h, t_ldp)
                        V.wait_ge(sDV.h, xnp_guard[0])
                        t_xp = sDV.inc(V.tensor_scalar(
                            xnp[:], xpb[b][:], str_[:], stb[:],
                            OP.mult, OP.add))
                        for e in (A, G):
                            e.wait_ge(sDV.h, t_xp)
                        if s >= 10:
                            V.wait_ge(sS.h, xqb_free[b])
                        emit_quant(ws, xnp[:, 0:2048], xqb[b][:, 0:2048],
                                   2048, 0)
                        t_q = emit_quant(ws, xnp[:, 2048:4096],
                                         xqb[b][:, 2048:4096], 2048, 128)
                        xpb_free[b] = t_q
                        xnp_guard[0] = t_q
                        SY.wait_ge(sDV.h, t_q)
                        r0 = (s - 8) * 128
                        xqb_free[b] = sS.inc(SY.dma_start(
                            out=xq_tm[r0:r0 + 128, :], in_=xqb[b][:]), 16)
            barrier()

            with ExitStack() as es:
                xn_fm = sb(es, "xn_fm", [128, DCH, 1024], BF16)
                wbuf = [sb(es, f"wbuf{i}", [128, DCH, 512], BF16)
                        for i in range(2)]
                cos_sb = sb(es, "cos_sb", [128, 16, HD], F32)
                sin_sb = sb(es, "sin_sb", [128, 16, HD], F32)
                A_sb = sb(es, "A_sb", [128, DCH, RLR], BF16)
                qe = [sb(es, f"qe{i}", [128, 512], F32) for i in range(2)]
                rot2 = [sb(es, f"rot{i}", [128, 4, HD], F32) for i in range(2)]
                tmA = sb(es, "tmA", [128, 4, HD], F32)
                qko = [sb(es, f"qko{i}", [128, 512], BF16) for i in range(2)]
                vout = [sb(es, f"vout{i}", [128, 512], BF16) for i in range(2)]
                pq = [psum(es, f"pq{i}", [128, 512], F32) for i in range(4)]
                pt1 = psum(es, "pt1", [64, 512], F32)

                sS.inc(SY.dma_start(
                    out=cos_sb[:],
                    in_=cos_d.rearrange("(c p) j -> p c j", p=128)), 16)
                sS.inc(SY.dma_start(
                    out=sin_sb[:],
                    in_=sin_d.rearrange("(c p) j -> p c j", p=128)), 16)
                t_cs = sS.n
                t_AB = sG.inc(G.dma_start(
                    out=A_sb[:],
                    in_=lrA.rearrange("(c p) j -> p c j", p=128)), 16)

                wbuf_free = [0, 0]
                pq_free = [0, 0, 0, 0]
                qe_free = [0, 0]
                qko_free = [0, 0]
                vout_free = [0, 0]
                rot_free = [0, 0]
                t_w = [0, 0]
                t_xn_all = max(xn_store)

                def load_wslice(wname, dw):
                    wsl = dw % 2
                    G.wait_ge(sPE.h, wbuf_free[wsl])
                    t_w[wsl] = sG.inc(G.dma_start(
                        out=wbuf[wsl][:],
                        in_=wT[wname][:, dw * 512:(dw + 1) * 512]
                        .rearrange("(c p) j -> p c j", p=128)), 16)

                for hf in range(2):
                    if hf == 1:
                        A.wait_ge(sPE.h, sPE.n)
                    for c in range(DCH):
                        A.wait_ge(sS.h, t_xn_all)
                        sC.inc(A.dma_start(
                            out=xn_fm[:, c, :],
                            in_=xn_tm[hf * 1024:(hf + 1) * 1024,
                                      c * 128:(c + 1) * 128],
                            transpose=True), 16)
                    t_xnf = sC.n
                    wlist = ["k", "v"] if hf == 0 else ["k", "v", "q"]
                    for wi, wname in enumerate(wlist):
                        tdst = {"k": k_tm, "v": v_tm, "q": q_tm}[wname]
                        load_wslice(wname, 0)
                        for dw in range(8):
                            wsl = dw % 2
                            t_wcur = t_w[wsl]
                            if dw < 7:
                                load_wslice(wname, dw + 1)
                            elif wi + 1 < len(wlist):
                                load_wslice(wlist[wi + 1], 0)
                            for tc in range(8):
                                psl = tc % 4
                                PE.wait_ge(sG.h, t_wcur)
                                PE.wait_ge(sC.h, t_xnf)
                                PE.wait_ge(sAC.h, pq_free[psl])
                                mm = None
                                for dc in range(DCH):
                                    mm = PE.matmul(
                                        pq[psl][:, :],
                                        xn_fm[:, dc, tc * 128:(tc + 1) * 128],
                                        wbuf[wsl][:, dc, :],
                                        start=(dc == 0), stop=(dc == DCH - 1))
                                t_mm = sPE.inc(mm)
                                if tc == 7:
                                    wbuf_free[wsl] = t_mm
                                rowg = hf * 1024 + tc * 128
                                esl = tc % 2
                                A.wait_ge(sPE.h, t_mm)
                                if wname == "v":
                                    A.wait_ge(sS.h, vout_free[esl])
                                    t_ev = sAC.inc(A.copy(vout[esl][:],
                                                          pq[psl][:, :]))
                                    pq_free[psl] = t_ev
                                    SY.wait_ge(sAC.h, t_ev)
                                    vout_free[esl] = sS.inc(SY.dma_start(
                                        out=v_tm[rowg:rowg + 128,
                                                 dw * 512:(dw + 1) * 512],
                                        in_=vout[esl][:]), 16)
                                else:
                                    A.wait_ge(sDV.h, qe_free[esl])
                                    t_ev = sAC.inc(A.copy(qe[esl][:],
                                                          pq[psl][:, :]))
                                    pq_free[psl] = t_ev
                                    qe3 = qe[esl][:].rearrange(
                                        "p (h d) -> p h d", d=HD)
                                    rot = rot2[esl]
                                    G.wait_ge(sAC.h, t_ev)
                                    G.wait_ge(sDV.h, rot_free[esl])
                                    G.tensor_scalar_mul(rot[:, :, 0:64],
                                                        qe3[:, :, 64:128], -1.0)
                                    t_rot = sPO.inc(G.tensor_copy(
                                        rot[:, :, 64:128], qe3[:, :, 0:64]))
                                    tcg = hf * 8 + tc
                                    cos_b = cos_sb[:, tcg, :].unsqueeze(1) \
                                        .broadcast_to([128, 4, HD])
                                    sin_b = sin_sb[:, tcg, :].unsqueeze(1) \
                                        .broadcast_to([128, 4, HD])
                                    V.wait_ge(sAC.h, t_ev)
                                    V.wait_ge(sS.h, t_cs)
                                    V.tensor_mul(tmA[:], qe3, cos_b)
                                    V.wait_ge(sPO.h, t_rot)
                                    V.wait_ge(sS.h, qko_free[esl])
                                    V.tensor_mul(rot[:], rot[:], sin_b)
                                    qk3 = qko[esl][:].rearrange(
                                        "p (h d) -> p h d", d=HD)
                                    t_rp = sDV.inc(V.tensor_add(qk3, tmA[:],
                                                                rot[:]))
                                    qe_free[esl] = t_rp
                                    rot_free[esl] = t_rp
                                    SY.wait_ge(sDV.h, t_rp)
                                    row = rowg if wname == "k" else tc * 128
                                    qko_free[esl] = sS.inc(SY.dma_start(
                                        out=tdst[row:row + 128,
                                                 dw * 512:(dw + 1) * 512],
                                        in_=qko[esl][:]), 16)
                    if hf == 1:
                        PE.wait_ge(sG.h, t_AB)
                        pt1_free = 0
                        for th in range(2):
                            PE.wait_ge(sAC.h, pt1_free)
                            mm = None
                            for dc in range(DCH):
                                mm = PE.matmul(
                                    pt1[:, :], A_sb[:, dc, :],
                                    xn_fm[:, dc, th * 512:(th + 1) * 512],
                                    start=(dc == 0), stop=(dc == DCH - 1))
                            t_mm = sPE.inc(mm)
                            A.wait_ge(sPE.h, t_mm)
                            pt1_free = sAC.inc(A.copy(
                                t1_bf[0:64, th * 512:(th + 1) * 512],
                                pt1[:, :]))
            barrier()

            with ExitStack() as es:
                wt_in = [sb(es, f"wt{i}", [128, D], F32) for i in range(2)]
                wq_bf = [sb(es, f"wqb{i}", [128, D], BF16) for i in range(2)]
                ws2 = [quant_ws(es, 2048, "_wqA"), quant_ws(es, 2048, "_wqB")]
                tstage = sb(es, "tstage", [128, DCH, 128], BF16)
                tstage_d = sb(es, "tstage_d", [128, FCH, 128], BF16)
                ps_tr = [psum(es, f"pstr{i}", [128, 1024], BF16)
                         for i in range(2)]
                units = []
                for wn, src_t in (("g", wg_sl), ("u", wu_sl)):
                    for i in range(11):
                        units.append((src_t[i * 128:(i + 1) * 128, :],
                                      D, 128, wn, i, 0))
                for i in range(4):
                    for (c0, w) in ((0, 4096), (4096, 4096), (8192, 2816)):
                        units.append((wd_sl[i * 128:(i + 1) * 128, c0:c0 + w],
                                      w, 128 if c0 == 8192 else 0,
                                      "d", i, c0 // 128))
                in_free = [0, 0]
                out_free = [0, 0]
                tr_free = [0, 0]
                stg_store = 0
                stgd_store = 0
                t_lds = [0, 0]
                t_lds[0] = sS.inc(SY.dma_start(out=wt_in[0][:, 0:units[0][1]],
                                               in_=units[0][0]), 16)
                for u, (usrc, n, sel, kind, rt, ctb) in enumerate(units):
                    b = u % 2
                    if u + 1 < len(units):
                        b2 = (u + 1) % 2
                        SY.wait_ge(sDV.h, in_free[b2])
                        nn = units[u + 1][1]
                        t_lds[b2] = sS.inc(SY.dma_start(
                            out=wt_in[b2][:, 0:nn], in_=units[u + 1][0]), 16)
                    for e in (V, A):
                        e.wait_ge(sS.h, t_lds[b])
                    V.wait_ge(sPE.h, out_free[b])
                    nh = n // 2
                    emit_quant(ws2[0], wt_in[b][:, 0:nh],
                               wq_bf[b][:, 0:nh], nh, 0)
                    t_q = emit_quant(ws2[1], wt_in[b][:, nh:n],
                                     wq_bf[b][:, nh:n], nh, sel)
                    in_free[b] = t_q
                    stg = tstage if kind != "d" else tstage_d
                    stg_tick = stg_store if kind != "d" else stgd_store
                    PE.wait_ge(sDV.h, ws2[0]["_prev"])
                    PE.wait_ge(sDV.h, t_q)
                    t_tr = 0
                    for j in range(n // 128):
                        sl = j % 2
                        PE.wait_ge(sAC.h, tr_free[sl])
                        t_tr = sPE.inc(PE.transpose(
                            ps_tr[sl][:, 0:128],
                            wq_bf[b][:, j * 128:(j + 1) * 128],
                            ident_sb[:]))
                        A.wait_ge(sPE.h, t_tr)
                        A.wait_ge(sS.h, stg_tick)
                        tr_free[sl] = sAC.inc(A.copy(stg[:, ctb + j, :],
                                                     ps_tr[sl][:, 0:128]))
                    out_free[b] = t_tr
                    if kind == "g":
                        dstT, full = pc_gT, True
                    elif kind == "u":
                        dstT, full = pc_uT, True
                    else:
                        dstT, full = pc_dT, (ctb == 64)
                    if full:
                        SY.wait_ge(sAC.h, sAC.n)
                        if kind == "d":
                            outap = pc_dT[:, rt * 128:(rt + 1) * 128] \
                                .rearrange("(c p) j -> p c j", p=128)
                            stgd_store = sS.inc(SY.dma_start(
                                out=outap, in_=tstage_d[:]), 16)
                        else:
                            outap = dstT[:, rt * 128:(rt + 1) * 128] \
                                .rearrange("(c p) j -> p c j", p=128)
                            stg_store = sS.inc(SY.dma_start(
                                out=outap, in_=tstage[:]), 16)
                G.wait_ge(sS.h, sS.n)
                sCCG.inc(G.collective_compute(
                    "AllGather", OP.bypass, replica_groups=[list(range(8))],
                    ins=[pc_gT], outs=[wgqT_g]))
                sCCU.inc(G.collective_compute(
                    "AllGather", OP.bypass, replica_groups=[list(range(8))],
                    ins=[pc_uT], outs=[wuqT_g]))
                sCCD.inc(G.collective_compute(
                    "AllGather", OP.bypass, replica_groups=[list(range(8))],
                    ins=[pc_dT], outs=[wdqT_g]))
            barrier(engs=[PE, V, A, SY])
            with ExitStack() as es3:
                masks_sb = sb(es3, "masks_sb", [128, 8, KSPAN], BF16)
                attn_fm = sb(es3, "attn_fm", [128, DCH, TOK], BF16)
                SY.wait_ge(sS.h, sS.n)
                for qt in range(8):
                    sS.inc(SY.dma_start(out=masks_sb[:, qt, :],
                                        in_=masks_d[qt]), 16)
                t_msk = sS.n

                with ExitStack() as es:
                    qh4 = [sb(es, f"qh4_{i}", [128, 4, TOK], BF16)
                           for i in range(2)]
                    kh4 = [sb(es, f"kh4_{i}", [128, 4, EXT], BF16)
                           for i in range(2)]
                    v4 = [sb(es, f"v4_{i}", [128, 16, 512], BF16)
                          for i in range(2)]
                    s1_sb = [sb(es, f"s1_sb{i}", [128, KSPAN], F32)
                             for i in range(2)]
                    p_f32 = [sb(es, f"p_f32{i}", [128, KSPAN], F32)
                             for i in range(2)]
                    p_bf = [sb(es, f"p_bf{i}", [128, KSPAN], BF16)
                            for i in range(2)]
                    pt_sb = [sb(es, f"pt_sb{i}", [128, KSPAN], BF16)
                             for i in range(2)]
                    mx = [sb(es, f"mx{i}", [128, 1], F32) for i in range(2)]
                    lsum = [sb(es, f"lsum{i}", [128, 1], F32)
                            for i in range(2)]
                    rl = [sb(es, f"rl{i}", [128, 1], F32) for i in range(2)]
                    ps_s = psum(es, "ps_s", [128, KSPAN], F32)
                    ps_pt = psum(es, "ps_pt", [128, KSPAN], BF16)
                    ps_o = [psum(es, f"ps_o{i}", [128, 512], F32)
                            for i in range(2)]

                    grp_free = [0, 0]
                    ps_s_free = [0]
                    ps_pt_free = [0]
                    ps_o_free = [0, 0]
                    s1_free = [0, 0]
                    pbf_free = [0, 0]
                    pt_free = [0, 0]
                    t_ldq = [0, 0]
                    t_ldv = [0, 0]
                    it_n = 0

                    def load_group(hg):
                        gsl = hg % 2
                        A.wait_ge(sPE.h, grp_free[gsl])
                        for i in range(4):
                            h = hg * 4 + i
                            sC.inc(A.dma_start(
                                out=qh4[gsl][:, i, :],
                                in_=q_tm[:, h * 128:(h + 1) * 128],
                                transpose=True), 16)
                            sC.inc(A.dma_start(
                                out=kh4[gsl][:, i, :],
                                in_=k_tm[:, h * 128:(h + 1) * 128],
                                transpose=True), 16)
                        t_ldq[gsl] = sC.n
                        SY.wait_ge(sPE.h, grp_free[gsl])
                        t_ldv[gsl] = sS.inc(SY.dma_start(
                            out=v4[gsl][:],
                            in_=v_tm[:, hg * 512:(hg + 1) * 512]
                            .rearrange("(c p) j -> p c j", p=128)), 16)

                    load_group(0)
                    load_group(1)
                    pend = [None]

                    def flush():
                        if pend[0] is None:
                            return
                        b, gsl, ch, qt, t_ex = pend[0]
                        pend[0] = None
                        V.wait_ge(sAC.h, t_ex)
                        V.wait_ge(sDV.h, sDV.inc(
                            V.reciprocal(rl[b][:], lsum[b][:])))
                        t_rl = sDV.inc(V.tensor_scalar_mul(
                            p_bf[b][:], p_f32[b][:], rl[b][:]))
                        pbf_w = t_rl
                        PE.wait_ge(sDV.h, pbf_w)
                        PE.wait_ge(sAC.h, ps_pt_free[0])
                        tr = None
                        for k in range(NKT):
                            tr = PE.transpose(
                                ps_pt[:, k * 128:(k + 1) * 128],
                                p_bf[b][:, k * 128:(k + 1) * 128],
                                ident_sb[:])
                        t_tr = sPE.inc(tr)
                        pbf_free[b] = t_tr
                        A.wait_ge(sPE.h, t_tr)
                        A.wait_ge(sPE.h, pt_free[b])
                        t_ptc = sAC.inc(A.copy(pt_sb[b][:], ps_pt[:, :]))
                        ps_pt_free[0] = t_ptc
                        PE.wait_ge(sAC.h, t_ptc)
                        PE.wait_ge(sS.h, t_ldv[gsl])
                        PE.wait_ge(sAC.h, ps_o_free[b])
                        mm = None
                        for k in range(NKT):
                            mm = PE.matmul(
                                ps_o[b][:, 0:128],
                                v4[gsl][:, qt + k, ch % 4 * 128:
                                        (ch % 4 + 1) * 128],
                                pt_sb[b][:, k * 128:(k + 1) * 128],
                                start=(k == 0), stop=(k == NKT - 1))
                        t_av = sPE.inc(mm)
                        pt_free[b] = t_av
                        grp_free[gsl] = t_av
                        A.wait_ge(sPE.h, t_av)
                        ps_o_free[b] = sAC.inc(A.copy(
                            attn_fm[:, ch, qt * 128:(qt + 1) * 128],
                            ps_o[b][:, 0:128]))

                    for hg in range(8):
                        gsl = hg % 2
                        for i in range(4):
                            for qt in range(8):
                                b = it_n % 2
                                it_n += 1
                                PE.wait_ge(sC.h, t_ldq[gsl])
                                PE.wait_ge(sDV.h, ps_s_free[0])
                                k0 = qt * 128
                                PE.matmul(ps_s[:, 0:512],
                                          qh4[gsl][:, i, k0:k0 + 128],
                                          kh4[gsl][:, i, k0:k0 + 512],
                                          start=True, stop=True)
                                PE.matmul(ps_s[:, 512:1024],
                                          qh4[gsl][:, i, k0:k0 + 128],
                                          kh4[gsl][:, i, k0 + 512:k0 + 1024],
                                          start=True, stop=True)
                                mm = PE.matmul(
                                    ps_s[:, 1024:1152],
                                    qh4[gsl][:, i, k0:k0 + 128],
                                    kh4[gsl][:, i, k0 + 1024:k0 + 1152],
                                    start=True, stop=True)
                                t_sc = sPE.inc(mm)
                                V.wait_ge(sPE.h, t_sc)
                                V.wait_ge(sS.h, t_msk)
                                V.wait_ge(sAC.h, s1_free[b])
                                V.wait_ge(sPE.h, pbf_free[b])
                                ps_s_free[0] = sDV.inc(V.scalar_tensor_tensor(
                                    s1_sb[b][:], ps_s[:, :], INVSQ,
                                    masks_sb[:, qt, :], OP.mult, OP.add))
                                t_mx = sDV.inc(V.reduce_max(
                                    mx[b][:], s1_sb[b][:], AX.X, negate=True))
                                A.wait_ge(sDV.h, t_mx)
                                t_ex = sAC.inc(A.activation(
                                    p_f32[b][:], s1_sb[b][:], AF.Exp,
                                    bias=mx[b][:], accum_out=lsum[b][:]))
                                s1_free[b] = t_ex
                                flush()
                                pend[0] = (b, gsl, hg * 4 + i, qt, t_ex)
                        flush()
                        if hg + 2 < 8:
                            load_group(hg + 2)
                barrier()

                with ExitStack() as es:
                    wo_buf = [sb(es, f"wo{i}", [128, DCH, 512], BF16)
                              for i in range(2)]
                    sout = [sb(es, f"sout{i}", [128, 512], BF16)
                            for i in range(2)]
                    po = [psum(es, f"po{i}", [128, 512], F32) for i in range(4)]
                    wo_free = [0, 0]
                    po_free = [0, 0, 0, 0]
                    so_free = [0, 0]
                    t_wo = [0, 0]

                    def load_wo(dw):
                        wsl = dw % 2
                        G.wait_ge(sPE.h, wo_free[wsl])
                        t_wo[wsl] = sG.inc(G.dma_start(
                            out=wo_buf[wsl][:],
                            in_=wT["o"][:, dw * 512:(dw + 1) * 512]
                            .rearrange("(c p) j -> p c j", p=128)), 16)

                    load_wo(0)
                    for dw in range(8):
                        wsl = dw % 2
                        if dw < 7:
                            load_wo(dw + 1)
                        for tc in range(8):
                            psl = tc % 4
                            PE.wait_ge(sG.h, t_wo[wsl])
                            PE.wait_ge(sAC.h, po_free[psl])
                            mm = None
                            for dc in range(DCH):
                                mm = PE.matmul(
                                    po[psl][:, :],
                                    attn_fm[:, dc, tc * 128:(tc + 1) * 128],
                                    wo_buf[wsl][:, dc, :],
                                    start=(dc == 0), stop=(dc == DCH - 1))
                            t_mm = sPE.inc(mm)
                            if tc == 7:
                                wo_free[wsl] = t_mm
                            esl = tc % 2
                            A.wait_ge(sPE.h, t_mm)
                            A.wait_ge(sS.h, so_free[esl])
                            t_ev = sAC.inc(A.copy(sout[esl][:], po[psl][:, :]))
                            po_free[psl] = t_ev
                            SY.wait_ge(sAC.h, t_ev)
                            so_free[esl] = sS.inc(SY.dma_start(
                                out=s_tm[tc * 128:(tc + 1) * 128,
                                         dw * 512:(dw + 1) * 512],
                                in_=sout[esl][:]), 16)
            barrier()


            with ExitStack() as es:
                xq_fm = sb(es, "xq_fm", [128, DCH, TOK], BF16)
                wring = {wn: [sb(es, f"wr_{wn}{i}", [128, 16, 512], BF16)
                              for i in range(3)] for wn in ("g", "u")}
                wsq = quant_ws(es, 512, "_h")
                sg_t = sb(es, "sg_t", [128, 512], F32)
                h_t = sb(es, "h_t", [128, 512], F32)
                hqb = [sb(es, f"hqb{i}", [128, 512], BF16) for i in range(2)]
                ps_g = [psum(es, f"psg{i}", [128, 512], F32) for i in range(2)]
                ps_u = [psum(es, f"psu{i}", [128, 512], F32) for i in range(2)]

                A.wait_ge(sS.h, sS.n)
                SY.wait_ge(sCCG.h, 1)
                SY.wait_ge(sCCU.h, 1)
                for c in range(DCH):
                    sC.inc(A.dma_start(
                        out=xq_fm[:, c, :],
                        in_=xq_tm[:, c * 128:(c + 1) * 128],
                        transpose=True), 16)
                t_xqf = sC.n
                if DEBUG_TAPS:
                    G.wait_ge(sC.h, t_xqf)
                    G.wait_ge(sG.h, sG.inc(G.dma_start(out=xqf_dbg,
                                                       in_=xq_fm[:]), 16))

                t_use = {}
                t_ld = {}

                def load_half(q):
                    fi, half = divmod(q, 2)
                    f0, fw = FWINS[fi]
                    for wn, gsrc in (("g", wgqT_g), ("u", wuqT_g)):
                        slot = wring[wn][q % 3]
                        prev = (wn, q - 3)
                        if prev in t_use:
                            SY.wait_ge(sPE.h, t_use[prev])
                        for lt, t in enumerate(range(f0 // 128,
                                                     (f0 + fw) // 128)):
                            r, lr = divmod(t, 11)
                            base = r * D + half * 2048
                            srcap = gsrc[base:base + 2048,
                                         lr * 128:(lr + 1) * 128] \
                                .rearrange("(c p) j -> p c j", p=128)
                            sS.inc(SY.dma_start(
                                out=slot[:, :, lt * 128:(lt + 1) * 128],
                                in_=srcap), 16)
                        t_ld[(wn, q)] = sS.n

                load_half(0)
                load_half(1)
                load_half(2)
                pg_free = [0, 0]
                pu_free = [0, 0]
                hq_free = [0, 0]
                sg_guard = 0
                hdbg_free = [0]
                for fi, (f0, fw) in enumerate(FWINS):
                    sel = 128 if fi == len(FWINS) - 1 else 0
                    for tt in range(8):
                        par = tt % 2
                        PE.wait_ge(sS.h, t_ld[("u", 2 * fi + 1)])
                        PE.wait_ge(sC.h, t_xqf)
                        PE.wait_ge(sAC.h, pg_free[par])
                        PE.wait_ge(sDV.h, pu_free[par])
                        for wn, pp in (("g", ps_g), ("u", ps_u)):
                            mm = None
                            for dc in range(DCH):
                                slot = wring[wn][(2 * fi + dc // 16) % 3]
                                mm = PE.matmul(
                                    pp[par][:, 0:fw],
                                    xq_fm[:, dc, tt * 128:(tt + 1) * 128],
                                    slot[:, dc % 16, 0:fw],
                                    start=(dc == 0), stop=(dc == DCH - 1))
                            sPE.inc(mm)
                        t_mm = sPE.n
                        A.wait_ge(sPE.h, t_mm)
                        A.wait_ge(sDV.h, sg_guard)
                        t_si = sAC.inc(A.activation(sg_t[:, 0:fw],
                                                    ps_g[par][:, 0:fw],
                                                    AF.Silu))
                        pg_free[par] = t_si
                        V.wait_ge(sAC.h, t_si)
                        if DEBUG_TAPS:
                            V.wait_ge(sG.h, hdbg_free[0])
                        t_h = sDV.inc(V.tensor_mul(h_t[:, 0:fw],
                                                   sg_t[:, 0:fw],
                                                   ps_u[par][:, 0:fw]))
                        pu_free[par] = t_h
                        sg_guard = t_h
                        if DEBUG_TAPS:
                            G.wait_ge(sDV.h, t_h)
                            hdbg_free[0] = sG.inc(G.dma_start(
                                out=h_dbg[tt * 128:(tt + 1) * 128,
                                          f0:f0 + fw],
                                in_=h_t[:, 0:fw]), 16)
                        for e in (G, A):
                            e.wait_ge(sDV.h, t_h)
                        hb = tt % 2
                        V.wait_ge(sG.h, hq_free[hb])
                        t_q = emit_quant(wsq, h_t[:, 0:fw], hqb[hb][:, 0:fw],
                                         fw, sel)
                        G.wait_ge(sDV.h, t_q)
                        hq_free[hb] = sG.inc(G.dma_start(
                            out=hq_tm[tt * 128:(tt + 1) * 128, f0:f0 + fw],
                            in_=hqb[hb][:, 0:fw]), 16)
                    t_use[("g", 2 * fi)] = sPE.n
                    t_use[("g", 2 * fi + 1)] = sPE.n
                    t_use[("u", 2 * fi)] = sPE.n
                    t_use[("u", 2 * fi + 1)] = sPE.n
                    if 2 * fi + 3 < 2 * len(FWINS):
                        load_half(2 * fi + 3)
                    if 2 * fi + 4 < 2 * len(FWINS):
                        load_half(2 * fi + 4)
                if DEBUG_TAPS:
                    G.wait_ge(sPE.h, sPE.n)
                    G.wait_ge(sG.h, sG.inc(G.dma_start(out=xqf_dbg2,
                                                       in_=xq_fm[:]), 16))
            barrier()

            with ExitStack() as es:
                hq_res = sb(es, "hq_res", [128, FCH, 512], BF16)
                NWSL = 8
                wd_rg = [sb(es, f"wd{i}", [128, 512], BF16) for i in range(NWSL)]
                scale_sb = sb(es, "scale_sb", [128, D], F32)
                B_sb = sb(es, "B_sb", [64, D], BF16)
                x_t = [sb(es, f"x_t{i}", [128, 512], F32) for i in range(2)]
                s_t = [sb(es, f"s_t{i}", [128, 512], BF16) for i in range(2)]
                g0 = sb(es, "g0", [128, 512], F32)
                g1 = sb(es, "g1", [128, 512], F32)
                g_bf = sb(es, "g_bf", [128, 512], BF16)
                m_t = sb(es, "m_t", [128, 512], F32)
                o_t = [sb(es, f"o_t{i}", [128, 512], F32) for i in range(2)]
                ps_d = [psum(es, f"psd{i}", [128, 512], F32) for i in range(8)]

                t_sc2 = sS.inc(SY.dma_start(out=scale_sb[:], in_=scale_rep), 16)
                t_B = sG.inc(G.dma_start(out=B_sb[:], in_=lrB), 16)

                wd_tick = {}
                wd_read = {}
                NTEN = 16 * FCH

                def load_wd(m):
                    dw_g, f = divmod(m, FCH)
                    dw = dw_g % 8
                    if m - NWSL in wd_read:
                        SY.wait_ge(sPE.h, wd_read[m - NWSL])
                    sS.inc(SY.dma_start(
                        out=wd_rg[m % NWSL][:],
                        in_=wdqT_g[dw * FF + f * 128:dw * FF + (f + 1) * 128,
                                   :]), 16)
                    wd_tick[m] = sS.n

                SY.wait_ge(sG.h, sG.n)
                SY.wait_ge(sCCD.h, 1)
                for m in range(NWSL):
                    load_wd(m)
                pd_free = [0] * 8
                x_free = [0, 0]
                o_free = [0, 0]
                for half in range(2):
                    A.wait_ge(sPE.h, sPE.n)
                    for f in range(FCH):
                        sC.inc(A.dma_start(
                            out=hq_res[:, f, :],
                            in_=hq_tm[half * 512:(half + 1) * 512,
                                      f * 128:(f + 1) * 128],
                            transpose=True), 16)
                    t_hq = sC.n
                    for dw in range(8):
                        m0 = (half * 8 + dw) * FCH
                        pbase = (dw % 2) * 4
                        for f in range(FCH):
                            m = m0 + f
                            PE.wait_ge(sS.h, wd_tick[m])
                            if f == 0:
                                PE.wait_ge(sC.h, t_hq)
                                for sl4 in range(4):
                                    PE.wait_ge(sDV.h, pd_free[pbase + sl4])
                            mm = None
                            for tt in range(4):
                                mm = PE.matmul(
                                    ps_d[pbase + tt][:, :],
                                    hq_res[:, f, tt * 128:(tt + 1) * 128],
                                    wd_rg[m % NWSL][:],
                                    start=(f == 0), stop=(f == FCH - 1))
                            wd_read[m] = sPE.inc(mm)
                            if m + NWSL < NTEN:
                                load_wd(m + NWSL)
                        t_dw = sPE.n
                        for tt in range(4):
                            sl = pbase + tt
                            rows = half * 512 + tt * 128
                            V.wait_ge(sPE.h, t_dw)
                            V.wait_ge(sS.h, t_sc2)
                            t_g0 = sDV.inc(V.tensor_mul(
                                g0[:], ps_d[sl][:, :],
                                scale_sb[:, dw * 512:(dw + 1) * 512]))
                            PE.wait_ge(sDV.h, t_g0)
                            PE.wait_ge(sG.h, t_B)
                            t_t2 = sPE.inc(PE.matmul(
                                ps_d[sl][:, :],
                                t1_bf[0:64, rows:rows + 128],
                                B_sb[0:64, dw * 512:(dw + 1) * 512],
                                start=True, stop=True))
                            V.wait_ge(sPE.h, t_t2)
                            t_g1 = sDV.inc(V.tensor_add(g1[:], g0[:],
                                                        ps_d[sl][:, :]))
                            pd_free[sl] = t_g1
                            A.wait_ge(sDV.h, t_g1)
                            t_gb = sAC.inc(A.activation(g_bf[:], g1[:],
                                                        AF.Silu))
                            bx = tt % 2
                            SY.wait_ge(sDV.h, x_free[bx])
                            t_x = sS.inc(SY.dma_start(
                                out=x_t[bx][:],
                                in_=x_ext[1024 + rows:1024 + rows + 128,
                                          dw * 512:(dw + 1) * 512]), 16)
                            t_s = sS.inc(SY.dma_start(
                                out=s_t[bx][:],
                                in_=s_tm[rows:rows + 128,
                                         dw * 512:(dw + 1) * 512]), 16)
                            V.wait_ge(sAC.h, t_gb)
                            V.wait_ge(sS.h, t_s)
                            V.tensor_mul(m_t[:], s_t[bx][:], g_bf[:])
                            bo = tt % 2
                            V.wait_ge(sOUT.h, o_free[bo])
                            t_o = sDV.inc(V.tensor_add(o_t[bo][:], m_t[:],
                                                       x_t[bx][:]))
                            x_free[bx] = t_o
                            SY.wait_ge(sDV.h, t_o)
                            o_free[bo] = sOUT.inc(SY.dma_start(
                                out=out_d[rows:rows + 128,
                                          dw * 512:(dw + 1) * 512],
                                in_=o_t[bo][:]), 16)
            SY.wait_ge(sOUT.h, 16 * 64)

    return nc


def stage_inputs(inputs):
    f32 = np.float32
    x = np.asarray(inputs["L0_attn_x"], dtype=f32)[0]
    up = np.asarray(inputs["L0_ffn_up_reorder_index"]).astype(np.int64)
    dn = np.asarray(inputs["L0_ffn_down_reorder_index"]).astype(np.int64)
    wq = np.asarray(inputs["L0_attn_Wq"], dtype=f32)
    wk = np.asarray(inputs["L0_attn_Wk"], dtype=f32)
    wv = np.asarray(inputs["L0_attn_Wv"], dtype=f32)
    wo = np.asarray(inputs["L0_attn_Wo"], dtype=f32)
    wg = np.asarray(inputs["L0_ffn_w_gate"], dtype=f32)
    wu = np.asarray(inputs["L0_ffn_w_up"], dtype=f32)
    wd = np.asarray(inputs["L0_ffn_w_down"], dtype=f32)
    mscale = np.asarray(inputs["L0_ffn_mlp_output_scale"], dtype=f32)
    lA = np.asarray(inputs["L0_ffn_lr_A"], dtype=f32)
    lB = np.asarray(inputs["L0_ffn_lr_B"], dtype=f32)

    wqT = np.ascontiguousarray(wq.T)
    wkT = np.ascontiguousarray(wk.T)
    wvT = np.ascontiguousarray(wv.T)
    woT = np.ascontiguousarray(wo.T)
    wg_pad = np.zeros((8 * 1408, D), dtype=f32)
    wg_pad[:FF] = wg[dn][:, up]
    wu_pad = np.zeros((8 * 1408, D), dtype=f32)
    wu_pad[:FF] = wu[dn][:, up]
    wd_s = np.ascontiguousarray(wd[:, dn])
    scale_rep = np.ascontiguousarray(np.broadcast_to(mscale, (128, D)))
    ident = np.eye(128, dtype=BF)
    lA_c = np.ascontiguousarray(lA)
    lB_c = np.ascontiguousarray(lB)

    inv = (1.0 / 10000.0 ** (np.arange(0, HD, 2, dtype=np.float64) / HD))

    in_maps = []
    for c in range(NCORES):
        if c == 0:
            x_ext = np.concatenate(
                [np.zeros((TOK, D), f32), x[0:TOK]], axis=0)
        else:
            x_ext = x[TOK * (c - 1):TOK * (c + 1)]
        xp_own = np.ascontiguousarray(x[TOK * c:TOK * (c + 1)][:, up])
        pos = np.arange(TOK * (c - 1), TOK * (c + 1), dtype=np.float64)
        ang = pos[:, None] * inv[None, :]
        emb = np.concatenate([ang, ang], axis=-1)
        cos_t = np.cos(emb).astype(f32)
        sin_t = np.sin(emb).astype(f32)
        qt_i = np.arange(8)[:, None, None]
        a_i = np.arange(128)[None, :, None]
        kk_i = np.arange(KSPAN)[None, None, :]
        qg = TOK * c + qt_i * 128 + a_i
        kg = TOK * (c - 1) + qt_i * 128 + kk_i
        allowed = (kg >= 0) & (kg <= qg) & (kg > qg - 1024)
        masks = np.where(allowed, np.float32(0.0),
                         np.float32(-1.0e30)).astype(BF)
        in_maps.append({
            "x_ext": np.ascontiguousarray(x_ext),
            "xp_own": xp_own,
            "wqT": wqT, "wkT": wkT, "wvT": wvT, "woT": woT,
            "wg_sl": wg_pad[1408 * c:1408 * (c + 1)],
            "wu_sl": wu_pad[1408 * c:1408 * (c + 1)],
            "wd_sl": wd_s[512 * c:512 * (c + 1)],
            "lrA": lA_c, "lrB": lB_c,
            "scale_rep": scale_rep,
            "cos_t": cos_t, "sin_t": sin_t,
            "masks": np.ascontiguousarray(masks),
            "ident": ident,
        })
    return in_maps


def kernel(**inputs):
    in_maps = stage_inputs(inputs)
    nc = build_nc()
    res = run_bass_kernel_spmd(nc, in_maps, core_ids=list(range(NCORES)))
    out = np.concatenate([res.results[i]["out"] for i in range(NCORES)],
                         axis=0)
    return out[None].astype(np.float32)


if __name__ == "__main__":
    nc = build_nc()
    print("built ok")


# version 14 (final)
# speedup vs baseline: 1.0922x; 1.0922x over previous
import sys
sys.path.insert(0, '/opt/trn_rl_repo')
import numpy as np
import ml_dtypes
from contextlib import ExitStack

import concourse.bass as bass
import concourse.mybir as mybir
from concourse.bass_utils import run_bass_kernel_spmd
from concourse.alu_op_type import AluOpType as OP

F32 = mybir.dt.float32
BF16 = mybir.dt.bfloat16
U8 = mybir.dt.uint8
I32 = mybir.dt.int32
AF = mybir.ActivationFunctionType
AX = mybir.AxisListType

D = 4096
NCORES = 8
TOK = 1024
EXT = 2048
HD = 128
FF = 11008
RLR = 64
DCH = D // 128
FCH = FF // 128
KSPAN = 1152
NKT = 9
MAGIC = 8388608.0
INVSQ = float(1.0 / np.sqrt(128.0))
BF = ml_dtypes.bfloat16
FWINS = [(i * 512, 512) for i in range(21)] + [(10752, 256)]
DEBUG_TAPS = False


class Sem:
    def __init__(self, nc, es, name):
        self.h = es.enter_context(nc.semaphore(name))
        self.n = 0

    def inc(self, instr, k=1):
        instr.then_inc(self.h, k)
        self.n += k
        return self.n


def build_nc():
    nc = bass.Bass("TRN2", target_bir_lowering=False)
    PE, V, A, G, SY = nc.tensor, nc.vector, nc.scalar, nc.gpsimd, nc.sync

    def din(name, shape, dt=F32):
        return nc.dram_tensor(name, shape, dt, kind="ExternalInput").ap()

    x_ext = din("x_ext", [EXT, D])
    xp_own = din("xp_own", [TOK, D])
    wT = {"q": din("wqT", [D, D]), "k": din("wkT", [D, D]),
          "v": din("wvT", [D, D]), "o": din("woT", [D, D])}
    wg_sl = din("wg_sl", [1408, D])
    wu_sl = din("wu_sl", [1408, D])
    wd_sl = din("wd_sl", [512, FF])
    lrA = din("lrA", [D, RLR])
    lrB = din("lrB", [RLR, D])
    scale_rep = din("scale_rep", [128, D])
    cos_d = din("cos_t", [EXT, HD])
    sin_d = din("sin_t", [EXT, HD])
    masks_d = din("masks", [8, 128, KSPAN], BF16)
    ident_d = din("ident", [128, 128], BF16)
    out_d = nc.dram_tensor("out", [TOK, D], F32, kind="ExternalOutput").ap()

    def scr(name, shape, dt=BF16):
        if DEBUG_TAPS:
            return nc.dram_tensor(name, shape, dt, kind="ExternalOutput").ap()
        return nc.dram_tensor(name, shape, dt).ap()

    xn_tm = scr("xn_tm", [EXT, D])
    xq_tm = scr("xq_tm", [TOK, D])
    q_tm = scr("q_tm", [TOK, D])
    k_tm = scr("k_tm", [EXT, D])
    v_tm = scr("v_tm", [EXT, D])
    s_tm = scr("s_tm", [TOK, D])
    pc_gT = nc.dram_tensor("pc_gT", [D, 1408], BF16).ap()
    pc_uT = nc.dram_tensor("pc_uT", [D, 1408], BF16).ap()
    pc_dT = nc.dram_tensor("pc_dT", [FF, 512], BF16).ap()
    wgqT_g = nc.dram_tensor("wgqT_g", [8 * D, 1408], BF16,
                            addr_space="Shared").ap()
    wuqT_g = nc.dram_tensor("wuqT_g", [8 * D, 1408], BF16,
                            addr_space="Shared").ap()
    wdqT_g = nc.dram_tensor("wdqT_g", [8 * FF, 512], BF16,
                            addr_space="Shared").ap()
    hq_tm = scr("hq_tm", [TOK, FF])
    h_dbg = scr("h_dbg", [TOK, FF], F32) if DEBUG_TAPS else None
    xqf_dbg = scr("xqf_dbg", [128, DCH, TOK]) if DEBUG_TAPS else None
    xqf_dbg2 = scr("xqf_dbg2", [128, DCH, TOK]) if DEBUG_TAPS else None

    es_top = ExitStack()
    sS = Sem(nc, es_top, "sS")
    sC = Sem(nc, es_top, "sC")
    sG = Sem(nc, es_top, "sG")
    sPE = Sem(nc, es_top, "sPE")
    sDV = Sem(nc, es_top, "sDV")
    sAC = Sem(nc, es_top, "sAC")
    sPO = Sem(nc, es_top, "sPO")
    sBAR = Sem(nc, es_top, "sBAR")
    sOUT = Sem(nc, es_top, "sOUT")
    sCCG = Sem(nc, es_top, "sCCG")
    sCCU = Sem(nc, es_top, "sCCU")
    sCCD = Sem(nc, es_top, "sCCD")
    ENGS = [PE, V, A, G, SY]

    def barrier(engs=None):
        engs = ENGS if engs is None else engs
        for e in engs:
            e.sem_inc(sBAR.h, 1)
        sBAR.n += len(engs)
        for e in engs:
            e.wait_ge(sBAR.h, sBAR.n)

    def sb(es, name, shape, dt):
        return es.enter_context(nc.sbuf_tensor(name, shape, dt))

    def psum(es, name, shape, dt=F32):
        return es.enter_context(nc.psum_tensor(name, shape, dt))

    def emit_quant(ws, x, outbf, n, sel_tail):
        nq = n - sel_tail
        nb = nq // 16
        qa, qt = ws["qa"], ws["qt"]
        gl1, gl2, vl1, sgn = ws["gl1"], ws["gl2"], ws["vl1"], ws["sgn"]
        bm, rec, sb6, qmu = ws["bm"], ws["rec"], ws["sb6"], ws["qmu"]
        x3 = x[:, 0:nq].rearrange("p (b k) -> p b k", k=16)
        tiny = nq <= 256

        def vop(instr):
            if tiny:
                V.wait_ge(sDV.h, sDV.inc(instr))
            return instr

        def aop(instr):
            if tiny:
                A.wait_ge(sAC.h, sAC.inc(instr))
            return instr

        if ws.get("_prev", 0):
            A.wait_ge(sDV.h, ws["_prev"])
        t_abs = sAC.inc(A.activation(qa[:, 0:nq], x[:, 0:nq], AF.Abs))
        t_sg = sAC.inc(A.sign(sgn[:, 0:nq], x[:, 0:nq]))
        t_bm = sDV.inc(V.tensor_reduce(bm[:, 0:nb], x3, AX.X, OP.max,
                                       apply_absolute_value=True))
        V.wait_ge(sDV.h, t_bm)
        V.wait_ge(sDV.h, sDV.inc(
            V.tensor_scalar_max(bm[:, 0:nb], bm[:, 0:nb], 1e-30)))
        V.reciprocal(rec[:, 0:nb], bm[:, 0:nb])
        V.wait_ge(sDV.h, sDV.inc(
            V.tensor_scalar_mul(sb6[:, 0:nb], bm[:, 0:nb], 1.0 / 6.0)))
        V.wait_ge(sAC.h, t_abs)
        rec_b = rec[:, 0:nb].unsqueeze(2).broadcast_to([128, nb, 16])
        sb6_b = sb6[:, 0:nb].unsqueeze(2).broadcast_to([128, nb, 16])
        t_t = sDV.inc(V.scalar_tensor_tensor(
            qt[:, 0:nq].rearrange("p (b k) -> p b k", k=16),
            qa[:, 0:nq].rearrange("p (b k) -> p b k", k=16),
            6.0, rec_b, OP.mult, OP.mult))
        A.wait_ge(sDV.h, t_t)
        aop(A.activation(gl1[:, 0:nq], qt[:, 0:nq], AF.Copy, bias=MAGIC))
        t_rn = sAC.inc(A.activation(gl1[:, 0:nq], gl1[:, 0:nq], AF.Copy,
                                    bias=-MAGIC))
        if tiny:
            A.wait_ge(sAC.h, t_rn)
        aop(A.activation(gl2[:, 0:nq], qt[:, 0:nq], AF.Copy, bias=MAGIC,
                         scale=2.0))
        t_hu = sAC.inc(A.activation(gl2[:, 0:nq], gl2[:, 0:nq], AF.Copy,
                                    bias=-MAGIC * 0.5, scale=0.5))
        if tiny:
            A.wait_ge(sAC.h, t_hu)
        V.wait_ge(sAC.h, t_rn)
        vop(V.tensor_scalar_min(gl1[:, 0:nq], gl1[:, 0:nq], 4.0))
        vop(V.tensor_scalar(vl1[:, 0:nq], qt[:, 0:nq], 5.0, 2.0,
                            OP.is_ge, OP.mult))
        vop(V.tensor_add(gl1[:, 0:nq], gl1[:, 0:nq], vl1[:, 0:nq]))
        vop(V.tensor_scalar(qmu[:, 0:nq], qt[:, 0:nq], 2.0, None, OP.is_lt))
        V.wait_ge(sAC.h, t_hu)
        vop(V.select(qt[:, 0:nq], qmu[:, 0:nq], gl2[:, 0:nq], gl1[:, 0:nq]))
        V.wait_ge(sAC.h, t_sg)
        vop(V.tensor_mul(qt[:, 0:nq], qt[:, 0:nq], sgn[:, 0:nq]))
        last = V.tensor_tensor(
            outbf[:, 0:nq].rearrange("p (b k) -> p b k", k=16),
            qt[:, 0:nq].rearrange("p (b k) -> p b k", k=16),
            sb6_b, OP.mult)
        if sel_tail:
            last = V.tensor_copy(outbf[:, nq:n], x[:, nq:n])
        ws["_prev"] = sDV.inc(last)
        return ws["_prev"]

    def quant_ws(es, n, tag):
        return {
            "qa": sb(es, "qa" + tag, [128, n], F32),
            "qt": sb(es, "qt" + tag, [128, n], F32),
            "gl1": sb(es, "gl1" + tag, [128, n], F32),
            "gl2": sb(es, "gl2" + tag, [128, n], F32),
            "vl1": sb(es, "vl1" + tag, [128, n], F32),
            "sgn": sb(es, "sgn" + tag, [128, n], F32),
            "qmu": sb(es, "qmu" + tag, [128, n], U8),
            "bm": sb(es, "bm" + tag, [128, n // 16], F32),
            "rec": sb(es, "rec" + tag, [128, n // 16], F32),
            "sb6": sb(es, "sb6" + tag, [128, n // 16], F32),
        }

    t1_bf = sb(es_top, "t1_bf", [64, TOK], BF16)
    ident_sb = sb(es_top, "ident_sb", [128, 128], BF16)
    cvals = [MAGIC, -MAGIC, -MAGIC * 0.5]
    cbuf = sb(es_top, "cbuf", [128, len(cvals)], F32)
    for i, v in enumerate(cvals):
        nc.const_aps.register(F32, v, cbuf[:, i:i + 1])
    xn_store = [0] * 16

    with es_top, nc.Block() as block:

        @block.sync
        def _(_sync):
            for i, v in enumerate(cvals):
                V.memset(cbuf[:, i:i + 1], v)
            t_id = sS.inc(SY.dma_start(out=ident_sb[:], in_=ident_d), 16)
            PE.wait_ge(sS.h, t_id)
            with ExitStack() as es:
                xb = [sb(es, f"xb{i}", [128, D], F32) for i in range(2)]
                xpb = [sb(es, f"xpb{i}", [128, D], F32) for i in range(2)]
                xnb = [sb(es, f"xnb{i}", [128, D], BF16) for i in range(2)]
                xqb = [sb(es, f"xqb{i}", [128, D], BF16) for i in range(2)]
                xnp = sb(es, "xnp", [128, D], F32)
                sqd = sb(es, "sqd", [128, D], F32)
                sqp = sb(es, "sqp", [128, 4], F32)
                st0 = sb(es, "st0", [128, 1], F32)
                st1 = sb(es, "st1", [128, 1], F32)
                stm = sb(es, "stm", [128, 1], F32)
                stv = sb(es, "stv", [128, 1], F32)
                sts = sb(es, "sts", [128, 1], F32)
                str_ = sb(es, "str", [128, 1], F32)
                stb = sb(es, "stb", [128, 1], F32)
                ws = quant_ws(es, 2048, "_ln")

                xb_free = [0, 0]
                xb_sq_free = [0, 0]
                xpb_free = [0, 0]
                xnb_free = [0, 0]
                xqb_free = [0, 0]
                xnp_guard = [0]

                for s in range(16):
                    own = s >= 8
                    b = s % 2
                    if s >= 2:
                        SY.wait_ge(sDV.h, xb_free[b])
                        SY.wait_ge(sAC.h, xb_sq_free[b])
                    t_ld = sS.inc(SY.dma_start(
                        out=xb[b][:], in_=x_ext[s * 128:(s + 1) * 128, :]), 16)
                    t_ldp = 0
                    if own:
                        if s >= 10:
                            SY.wait_ge(sDV.h, xpb_free[b])
                        r0 = (s - 8) * 128
                        t_ldp = sS.inc(SY.dma_start(
                            out=xpb[b][:], in_=xp_own[r0:r0 + 128, :]), 16)
                    V.wait_ge(sS.h, t_ld)
                    t_s0 = sDV.inc(V.reduce_sum(st0[:], xb[b][:], AX.X))
                    A.wait_ge(sS.h, t_ld)
                    t_s1 = 0
                    for q4 in range(4):
                        t_s1 = sAC.inc(A.activation(
                            sqd[:, q4 * 1024:(q4 + 1) * 1024],
                            xb[b][:, q4 * 1024:(q4 + 1) * 1024],
                            AF.Square, accum_out=sqp[:, q4:q4 + 1]))
                    xb_sq_free[b] = t_s1
                    V.wait_ge(sAC.h, t_s1)
                    t_s2 = sDV.inc(V.reduce_sum(st1[:], sqp[:], AX.X))
                    V.wait_ge(sDV.h, t_s2)
                    V.wait_ge(sDV.h, t_s0)

                    def vtiny(instr):
                        V.wait_ge(sDV.h, sDV.inc(instr))

                    vtiny(V.tensor_scalar_mul(stm[:], st0[:], 1.0 / D))
                    vtiny(V.tensor_scalar(stv[:], st1[:], 1.0 / D, 1e-5,
                                          OP.mult, OP.add))
                    vtiny(V.tensor_mul(sts[:], stm[:], stm[:]))
                    t_var = sDV.inc(V.tensor_sub(stv[:], stv[:], sts[:]))
                    A.wait_ge(sDV.h, t_var)
                    t_sqr = sAC.inc(A.activation(sts[:], stv[:], AF.Sqrt))
                    V.wait_ge(sAC.h, t_sqr)
                    vtiny(V.reciprocal(str_[:], sts[:]))
                    t_rs = sDV.inc(V.scalar_tensor_tensor(
                        stb[:], stm[:], -1.0, str_[:], OP.mult, OP.mult))
                    V.wait_ge(sDV.h, t_rs)
                    if s >= 2:
                        V.wait_ge(sS.h, xnb_free[b])
                    t_xn = sDV.inc(V.tensor_scalar(
                        xnb[b][:], xb[b][:], str_[:], stb[:],
                        OP.mult, OP.add))
                    xb_free[b] = t_xn
                    SY.wait_ge(sDV.h, t_xn)
                    xn_store[s] = sS.inc(SY.dma_start(
                        out=xn_tm[s * 128:(s + 1) * 128, :], in_=xnb[b][:]), 16)
                    xnb_free[b] = xn_store[s]
                    if own:
                        V.wait_ge(sS.h, t_ldp)
                        V.wait_ge(sDV.h, xnp_guard[0])
                        t_xp = sDV.inc(V.tensor_scalar(
                            xnp[:], xpb[b][:], str_[:], stb[:],
                            OP.mult, OP.add))
                        for e in (A, G):
                            e.wait_ge(sDV.h, t_xp)
                        if s >= 10:
                            V.wait_ge(sS.h, xqb_free[b])
                        emit_quant(ws, xnp[:, 0:2048], xqb[b][:, 0:2048],
                                   2048, 0)
                        t_q = emit_quant(ws, xnp[:, 2048:4096],
                                         xqb[b][:, 2048:4096], 2048, 128)
                        xpb_free[b] = t_q
                        xnp_guard[0] = t_q
                        SY.wait_ge(sDV.h, t_q)
                        r0 = (s - 8) * 128
                        xqb_free[b] = sS.inc(SY.dma_start(
                            out=xq_tm[r0:r0 + 128, :], in_=xqb[b][:]), 16)
            barrier()

            with ExitStack() as es:
                xn_fm = sb(es, "xn_fm", [128, DCH, 1024], BF16)
                wbuf = [sb(es, f"wbuf{i}", [128, DCH, 512], BF16)
                        for i in range(2)]
                cos_sb = sb(es, "cos_sb", [128, 16, HD], F32)
                sin_sb = sb(es, "sin_sb", [128, 16, HD], F32)
                A_sb = sb(es, "A_sb", [128, DCH, RLR], BF16)
                qe = [sb(es, f"qe{i}", [128, 512], F32) for i in range(2)]
                rot2 = [sb(es, f"rot{i}", [128, 4, HD], F32) for i in range(2)]
                tmA = sb(es, "tmA", [128, 4, HD], F32)
                qko = [sb(es, f"qko{i}", [128, 512], BF16) for i in range(2)]
                vout = [sb(es, f"vout{i}", [128, 512], BF16) for i in range(2)]
                pq = [psum(es, f"pq{i}", [128, 512], F32) for i in range(4)]
                pt1 = psum(es, "pt1", [64, 512], F32)

                sS.inc(SY.dma_start(
                    out=cos_sb[:],
                    in_=cos_d.rearrange("(c p) j -> p c j", p=128)), 16)
                sS.inc(SY.dma_start(
                    out=sin_sb[:],
                    in_=sin_d.rearrange("(c p) j -> p c j", p=128)), 16)
                t_cs = sS.n
                t_AB = sG.inc(G.dma_start(
                    out=A_sb[:],
                    in_=lrA.rearrange("(c p) j -> p c j", p=128)), 16)

                wbuf_free = [0, 0]
                pq_free = [0, 0, 0, 0]
                qe_free = [0, 0]
                qko_free = [0, 0]
                vout_free = [0, 0]
                rot_free = [0, 0]
                t_w = [0, 0]
                t_xn_all = max(xn_store)

                def load_wslice(wname, dw):
                    wsl = dw % 2
                    G.wait_ge(sPE.h, wbuf_free[wsl])
                    t_w[wsl] = sG.inc(G.dma_start(
                        out=wbuf[wsl][:],
                        in_=wT[wname][:, dw * 512:(dw + 1) * 512]
                        .rearrange("(c p) j -> p c j", p=128)), 16)

                for hf in range(2):
                    if hf == 1:
                        A.wait_ge(sPE.h, sPE.n)
                    for c in range(DCH):
                        A.wait_ge(sS.h, t_xn_all)
                        sC.inc(A.dma_start(
                            out=xn_fm[:, c, :],
                            in_=xn_tm[hf * 1024:(hf + 1) * 1024,
                                      c * 128:(c + 1) * 128],
                            transpose=True), 16)
                    t_xnf = sC.n
                    wlist = ["k", "v"] if hf == 0 else ["k", "v", "q"]
                    for wi, wname in enumerate(wlist):
                        tdst = {"k": k_tm, "v": v_tm, "q": q_tm}[wname]
                        if wi == 0:
                            load_wslice(wname, 0)
                        for dw in range(8):
                            wsl = dw % 2
                            t_wcur = t_w[wsl]
                            if dw < 7:
                                load_wslice(wname, dw + 1)
                            elif wi + 1 < len(wlist):
                                load_wslice(wlist[wi + 1], 0)
                            for tc in range(8):
                                psl = tc % 4
                                PE.wait_ge(sG.h, t_wcur)
                                PE.wait_ge(sC.h, t_xnf)
                                PE.wait_ge(sAC.h, pq_free[psl])
                                mm = None
                                for dc in range(DCH):
                                    mm = PE.matmul(
                                        pq[psl][:, :],
                                        xn_fm[:, dc, tc * 128:(tc + 1) * 128],
                                        wbuf[wsl][:, dc, :],
                                        start=(dc == 0), stop=(dc == DCH - 1))
                                t_mm = sPE.inc(mm)
                                if tc == 7:
                                    wbuf_free[wsl] = t_mm
                                rowg = hf * 1024 + tc * 128
                                esl = tc % 2
                                A.wait_ge(sPE.h, t_mm)
                                if wname == "v":
                                    A.wait_ge(sS.h, vout_free[esl])
                                    t_ev = sAC.inc(A.copy(vout[esl][:],
                                                          pq[psl][:, :]))
                                    pq_free[psl] = t_ev
                                    SY.wait_ge(sAC.h, t_ev)
                                    vout_free[esl] = sS.inc(SY.dma_start(
                                        out=v_tm[rowg:rowg + 128,
                                                 dw * 512:(dw + 1) * 512],
                                        in_=vout[esl][:]), 16)
                                else:
                                    A.wait_ge(sDV.h, qe_free[esl])
                                    t_ev = sAC.inc(A.copy(qe[esl][:],
                                                          pq[psl][:, :]))
                                    pq_free[psl] = t_ev
                                    qe3 = qe[esl][:].rearrange(
                                        "p (h d) -> p h d", d=HD)
                                    rot = rot2[esl]
                                    G.wait_ge(sAC.h, t_ev)
                                    G.wait_ge(sDV.h, rot_free[esl])
                                    G.tensor_scalar_mul(rot[:, :, 0:64],
                                                        qe3[:, :, 64:128], -1.0)
                                    t_rot = sPO.inc(G.tensor_copy(
                                        rot[:, :, 64:128], qe3[:, :, 0:64]))
                                    tcg = hf * 8 + tc
                                    cos_b = cos_sb[:, tcg, :].unsqueeze(1) \
                                        .broadcast_to([128, 4, HD])
                                    sin_b = sin_sb[:, tcg, :].unsqueeze(1) \
                                        .broadcast_to([128, 4, HD])
                                    V.wait_ge(sAC.h, t_ev)
                                    V.wait_ge(sS.h, t_cs)
                                    V.tensor_mul(tmA[:], qe3, cos_b)
                                    V.wait_ge(sPO.h, t_rot)
                                    V.wait_ge(sS.h, qko_free[esl])
                                    V.tensor_mul(rot[:], rot[:], sin_b)
                                    qk3 = qko[esl][:].rearrange(
                                        "p (h d) -> p h d", d=HD)
                                    t_rp = sDV.inc(V.tensor_add(qk3, tmA[:],
                                                                rot[:]))
                                    qe_free[esl] = t_rp
                                    rot_free[esl] = t_rp
                                    SY.wait_ge(sDV.h, t_rp)
                                    row = rowg if wname == "k" else tc * 128
                                    qko_free[esl] = sS.inc(SY.dma_start(
                                        out=tdst[row:row + 128,
                                                 dw * 512:(dw + 1) * 512],
                                        in_=qko[esl][:]), 16)
                    if hf == 1:
                        PE.wait_ge(sG.h, t_AB)
                        pt1_free = 0
                        for th in range(2):
                            PE.wait_ge(sAC.h, pt1_free)
                            mm = None
                            for dc in range(DCH):
                                mm = PE.matmul(
                                    pt1[:, :], A_sb[:, dc, :],
                                    xn_fm[:, dc, th * 512:(th + 1) * 512],
                                    start=(dc == 0), stop=(dc == DCH - 1))
                            t_mm = sPE.inc(mm)
                            A.wait_ge(sPE.h, t_mm)
                            pt1_free = sAC.inc(A.copy(
                                t1_bf[0:64, th * 512:(th + 1) * 512],
                                pt1[:, :]))
            barrier()

            with ExitStack() as es:
                wt_in = [sb(es, f"wt{i}", [128, D], F32) for i in range(2)]
                wq_bf = [sb(es, f"wqb{i}", [128, D], BF16) for i in range(2)]
                ws2 = [quant_ws(es, 2048, "_wqA"), quant_ws(es, 2048, "_wqB")]
                tstage = sb(es, "tstage", [128, DCH, 128], BF16)
                tstage_d = sb(es, "tstage_d", [128, FCH, 128], BF16)
                ps_tr = [psum(es, f"pstr{i}", [128, 1024], BF16)
                         for i in range(2)]
                units = []
                for wn, src_t in (("g", wg_sl), ("u", wu_sl)):
                    for i in range(11):
                        units.append((src_t[i * 128:(i + 1) * 128, :],
                                      D, 128, wn, i, 0))
                for i in range(4):
                    for (c0, w) in ((0, 4096), (4096, 4096), (8192, 2816)):
                        units.append((wd_sl[i * 128:(i + 1) * 128, c0:c0 + w],
                                      w, 128 if c0 == 8192 else 0,
                                      "d", i, c0 // 128))
                in_free = [0, 0]
                out_free = [0, 0]
                tr_free = [0, 0]
                stg_store = 0
                stgd_store = 0
                t_lds = [0, 0]
                t_lds[0] = sS.inc(SY.dma_start(out=wt_in[0][:, 0:units[0][1]],
                                               in_=units[0][0]), 16)
                for u, (usrc, n, sel, kind, rt, ctb) in enumerate(units):
                    b = u % 2
                    if u + 1 < len(units):
                        b2 = (u + 1) % 2
                        SY.wait_ge(sDV.h, in_free[b2])
                        nn = units[u + 1][1]
                        t_lds[b2] = sS.inc(SY.dma_start(
                            out=wt_in[b2][:, 0:nn], in_=units[u + 1][0]), 16)
                    for e in (V, A):
                        e.wait_ge(sS.h, t_lds[b])
                    V.wait_ge(sPE.h, out_free[b])
                    nh = n // 2
                    emit_quant(ws2[0], wt_in[b][:, 0:nh],
                               wq_bf[b][:, 0:nh], nh, 0)
                    t_q = emit_quant(ws2[1], wt_in[b][:, nh:n],
                                     wq_bf[b][:, nh:n], nh, sel)
                    in_free[b] = t_q
                    stg = tstage if kind != "d" else tstage_d
                    stg_tick = stg_store if kind != "d" else stgd_store
                    PE.wait_ge(sDV.h, ws2[0]["_prev"])
                    PE.wait_ge(sDV.h, t_q)
                    t_tr = 0
                    for j in range(n // 128):
                        sl = j % 2
                        PE.wait_ge(sAC.h, tr_free[sl])
                        t_tr = sPE.inc(PE.transpose(
                            ps_tr[sl][:, 0:128],
                            wq_bf[b][:, j * 128:(j + 1) * 128],
                            ident_sb[:]))
                        A.wait_ge(sPE.h, t_tr)
                        A.wait_ge(sS.h, stg_tick)
                        tr_free[sl] = sAC.inc(A.copy(stg[:, ctb + j, :],
                                                     ps_tr[sl][:, 0:128]))
                    out_free[b] = t_tr
                    if kind == "g":
                        dstT, full = pc_gT, True
                    elif kind == "u":
                        dstT, full = pc_uT, True
                    else:
                        dstT, full = pc_dT, (ctb == 64)
                    if full:
                        SY.wait_ge(sAC.h, sAC.n)
                        if kind == "d":
                            outap = pc_dT[:, rt * 128:(rt + 1) * 128] \
                                .rearrange("(c p) j -> p c j", p=128)
                            stgd_store = sS.inc(SY.dma_start(
                                out=outap, in_=tstage_d[:]), 16)
                        else:
                            outap = dstT[:, rt * 128:(rt + 1) * 128] \
                                .rearrange("(c p) j -> p c j", p=128)
                            stg_store = sS.inc(SY.dma_start(
                                out=outap, in_=tstage[:]), 16)
                G.wait_ge(sS.h, sS.n)
                sCCG.inc(G.collective_compute(
                    "AllGather", OP.bypass, replica_groups=[list(range(8))],
                    ins=[pc_gT], outs=[wgqT_g]))
                sCCU.inc(G.collective_compute(
                    "AllGather", OP.bypass, replica_groups=[list(range(8))],
                    ins=[pc_uT], outs=[wuqT_g]))
                sCCD.inc(G.collective_compute(
                    "AllGather", OP.bypass, replica_groups=[list(range(8))],
                    ins=[pc_dT], outs=[wdqT_g]))
            barrier(engs=[PE, V, A, SY])
            with ExitStack() as es3:
                masks_sb = sb(es3, "masks_sb", [128, 8, KSPAN], BF16)
                attn_fm = sb(es3, "attn_fm", [128, DCH, TOK], BF16)
                SY.wait_ge(sS.h, sS.n)
                for qt in range(8):
                    sS.inc(SY.dma_start(out=masks_sb[:, qt, :],
                                        in_=masks_d[qt]), 16)
                t_msk = sS.n

                with ExitStack() as es:
                    qh4 = [sb(es, f"qh4_{i}", [128, 4, TOK], BF16)
                           for i in range(2)]
                    kh4 = [sb(es, f"kh4_{i}", [128, 4, EXT], BF16)
                           for i in range(2)]
                    v4 = [sb(es, f"v4_{i}", [128, 16, 512], BF16)
                          for i in range(2)]
                    s1_sb = [sb(es, f"s1_sb{i}", [128, KSPAN], F32)
                             for i in range(2)]
                    p_f32 = [sb(es, f"p_f32{i}", [128, KSPAN], F32)
                             for i in range(2)]
                    p_bf = [sb(es, f"p_bf{i}", [128, KSPAN], BF16)
                            for i in range(2)]
                    pt_sb = [sb(es, f"pt_sb{i}", [128, KSPAN], BF16)
                             for i in range(2)]
                    mx = [sb(es, f"mx{i}", [128, 1], F32) for i in range(2)]
                    lsum = [sb(es, f"lsum{i}", [128, 1], F32)
                            for i in range(2)]
                    rl = [sb(es, f"rl{i}", [128, 1], F32) for i in range(2)]
                    ps_s = psum(es, "ps_s", [128, KSPAN], F32)
                    ps_pt = psum(es, "ps_pt", [128, KSPAN], BF16)
                    ps_o = [psum(es, f"ps_o{i}", [128, 512], F32)
                            for i in range(2)]

                    grp_free = [0, 0]
                    ps_s_free = [0]
                    ps_pt_free = [0]
                    ps_o_free = [0, 0]
                    s1_free = [0, 0]
                    pbf_free = [0, 0]
                    pt_free = [0, 0]
                    t_ldq = [0, 0]
                    t_ldv = [0, 0]
                    it_n = 0

                    def load_group(hg):
                        gsl = hg % 2
                        A.wait_ge(sPE.h, grp_free[gsl])
                        for i in range(4):
                            h = hg * 4 + i
                            sC.inc(A.dma_start(
                                out=qh4[gsl][:, i, :],
                                in_=q_tm[:, h * 128:(h + 1) * 128],
                                transpose=True), 16)
                            sC.inc(A.dma_start(
                                out=kh4[gsl][:, i, :],
                                in_=k_tm[:, h * 128:(h + 1) * 128],
                                transpose=True), 16)
                        t_ldq[gsl] = sC.n
                        SY.wait_ge(sPE.h, grp_free[gsl])
                        t_ldv[gsl] = sS.inc(SY.dma_start(
                            out=v4[gsl][:],
                            in_=v_tm[:, hg * 512:(hg + 1) * 512]
                            .rearrange("(c p) j -> p c j", p=128)), 16)

                    load_group(0)
                    load_group(1)
                    pend = [None]

                    def flush():
                        if pend[0] is None:
                            return
                        b, gsl, ch, qt, t_ex = pend[0]
                        pend[0] = None
                        V.wait_ge(sAC.h, t_ex)
                        V.wait_ge(sDV.h, sDV.inc(
                            V.reciprocal(rl[b][:], lsum[b][:])))
                        t_rl = sDV.inc(V.tensor_scalar_mul(
                            p_bf[b][:], p_f32[b][:], rl[b][:]))
                        pbf_w = t_rl
                        PE.wait_ge(sDV.h, pbf_w)
                        PE.wait_ge(sAC.h, ps_pt_free[0])
                        tr = None
                        for k in range(NKT):
                            tr = PE.transpose(
                                ps_pt[:, k * 128:(k + 1) * 128],
                                p_bf[b][:, k * 128:(k + 1) * 128],
                                ident_sb[:])
                        t_tr = sPE.inc(tr)
                        pbf_free[b] = t_tr
                        A.wait_ge(sPE.h, t_tr)
                        A.wait_ge(sPE.h, pt_free[b])
                        t_ptc = sAC.inc(A.copy(pt_sb[b][:], ps_pt[:, :]))
                        ps_pt_free[0] = t_ptc
                        PE.wait_ge(sAC.h, t_ptc)
                        PE.wait_ge(sS.h, t_ldv[gsl])
                        PE.wait_ge(sAC.h, ps_o_free[b])
                        mm = None
                        for k in range(NKT):
                            mm = PE.matmul(
                                ps_o[b][:, 0:128],
                                v4[gsl][:, qt + k, ch % 4 * 128:
                                        (ch % 4 + 1) * 128],
                                pt_sb[b][:, k * 128:(k + 1) * 128],
                                start=(k == 0), stop=(k == NKT - 1))
                        t_av = sPE.inc(mm)
                        pt_free[b] = t_av
                        grp_free[gsl] = t_av
                        A.wait_ge(sPE.h, t_av)
                        ps_o_free[b] = sAC.inc(A.copy(
                            attn_fm[:, ch, qt * 128:(qt + 1) * 128],
                            ps_o[b][:, 0:128]))

                    for hg in range(8):
                        gsl = hg % 2
                        for i in range(4):
                            for qt in range(8):
                                b = it_n % 2
                                it_n += 1
                                PE.wait_ge(sC.h, t_ldq[gsl])
                                PE.wait_ge(sDV.h, ps_s_free[0])
                                k0 = qt * 128
                                PE.matmul(ps_s[:, 0:512],
                                          qh4[gsl][:, i, k0:k0 + 128],
                                          kh4[gsl][:, i, k0:k0 + 512],
                                          start=True, stop=True)
                                PE.matmul(ps_s[:, 512:1024],
                                          qh4[gsl][:, i, k0:k0 + 128],
                                          kh4[gsl][:, i, k0 + 512:k0 + 1024],
                                          start=True, stop=True)
                                mm = PE.matmul(
                                    ps_s[:, 1024:1152],
                                    qh4[gsl][:, i, k0:k0 + 128],
                                    kh4[gsl][:, i, k0 + 1024:k0 + 1152],
                                    start=True, stop=True)
                                t_sc = sPE.inc(mm)
                                V.wait_ge(sPE.h, t_sc)
                                V.wait_ge(sS.h, t_msk)
                                V.wait_ge(sAC.h, s1_free[b])
                                V.wait_ge(sPE.h, pbf_free[b])
                                ps_s_free[0] = sDV.inc(V.scalar_tensor_tensor(
                                    s1_sb[b][:], ps_s[:, :], INVSQ,
                                    masks_sb[:, qt, :], OP.mult, OP.add))
                                t_mx = sDV.inc(V.reduce_max(
                                    mx[b][:], s1_sb[b][:], AX.X, negate=True))
                                A.wait_ge(sDV.h, t_mx)
                                t_ex = sAC.inc(A.activation(
                                    p_f32[b][:], s1_sb[b][:], AF.Exp,
                                    bias=mx[b][:], accum_out=lsum[b][:]))
                                s1_free[b] = t_ex
                                flush()
                                pend[0] = (b, gsl, hg * 4 + i, qt, t_ex)
                        flush()
                        if hg + 2 < 8:
                            load_group(hg + 2)
                barrier()

                with ExitStack() as es:
                    wo_buf = [sb(es, f"wo{i}", [128, DCH, 512], BF16)
                              for i in range(2)]
                    sout = [sb(es, f"sout{i}", [128, 512], BF16)
                            for i in range(2)]
                    po = [psum(es, f"po{i}", [128, 512], F32) for i in range(4)]
                    wo_free = [0, 0]
                    po_free = [0, 0, 0, 0]
                    so_free = [0, 0]
                    t_wo = [0, 0]

                    def load_wo(dw):
                        wsl = dw % 2
                        G.wait_ge(sPE.h, wo_free[wsl])
                        t_wo[wsl] = sG.inc(G.dma_start(
                            out=wo_buf[wsl][:],
                            in_=wT["o"][:, dw * 512:(dw + 1) * 512]
                            .rearrange("(c p) j -> p c j", p=128)), 16)

                    load_wo(0)
                    for dw in range(8):
                        wsl = dw % 2
                        if dw < 7:
                            load_wo(dw + 1)
                        for tc in range(8):
                            psl = tc % 4
                            PE.wait_ge(sG.h, t_wo[wsl])
                            PE.wait_ge(sAC.h, po_free[psl])
                            mm = None
                            for dc in range(DCH):
                                mm = PE.matmul(
                                    po[psl][:, :],
                                    attn_fm[:, dc, tc * 128:(tc + 1) * 128],
                                    wo_buf[wsl][:, dc, :],
                                    start=(dc == 0), stop=(dc == DCH - 1))
                            t_mm = sPE.inc(mm)
                            if tc == 7:
                                wo_free[wsl] = t_mm
                            esl = tc % 2
                            A.wait_ge(sPE.h, t_mm)
                            A.wait_ge(sS.h, so_free[esl])
                            t_ev = sAC.inc(A.copy(sout[esl][:], po[psl][:, :]))
                            po_free[psl] = t_ev
                            SY.wait_ge(sAC.h, t_ev)
                            so_free[esl] = sS.inc(SY.dma_start(
                                out=s_tm[tc * 128:(tc + 1) * 128,
                                         dw * 512:(dw + 1) * 512],
                                in_=sout[esl][:]), 16)
            barrier()


            with ExitStack() as es:
                xq_fm = sb(es, "xq_fm", [128, DCH, TOK], BF16)
                wring = {wn: [sb(es, f"wr_{wn}{i}", [128, 16, 512], BF16)
                              for i in range(3)] for wn in ("g", "u")}
                wsq = quant_ws(es, 512, "_h")
                sg_t = sb(es, "sg_t", [128, 512], F32)
                h_t = sb(es, "h_t", [128, 512], F32)
                hqb = [sb(es, f"hqb{i}", [128, 512], BF16) for i in range(2)]
                ps_g = [psum(es, f"psg{i}", [128, 512], F32) for i in range(4)]
                ps_u = [psum(es, f"psu{i}", [128, 512], F32) for i in range(4)]

                A.wait_ge(sS.h, sS.n)
                SY.wait_ge(sCCG.h, 1)
                SY.wait_ge(sCCU.h, 1)
                for c in range(DCH):
                    sC.inc(A.dma_start(
                        out=xq_fm[:, c, :],
                        in_=xq_tm[:, c * 128:(c + 1) * 128],
                        transpose=True), 16)
                t_xqf = sC.n
                if DEBUG_TAPS:
                    G.wait_ge(sC.h, t_xqf)
                    G.wait_ge(sG.h, sG.inc(G.dma_start(out=xqf_dbg,
                                                       in_=xq_fm[:]), 16))

                t_use = {}
                t_ld = {}

                def load_half(q):
                    fi, half = divmod(q, 2)
                    f0, fw = FWINS[fi]
                    for wn, gsrc in (("g", wgqT_g), ("u", wuqT_g)):
                        slot = wring[wn][q % 3]
                        prev = (wn, q - 3)
                        if prev in t_use:
                            SY.wait_ge(sPE.h, t_use[prev])
                        for lt, t in enumerate(range(f0 // 128,
                                                     (f0 + fw) // 128)):
                            r, lr = divmod(t, 11)
                            base = r * D + half * 2048
                            srcap = gsrc[base:base + 2048,
                                         lr * 128:(lr + 1) * 128] \
                                .rearrange("(c p) j -> p c j", p=128)
                            sS.inc(SY.dma_start(
                                out=slot[:, :, lt * 128:(lt + 1) * 128],
                                in_=srcap), 16)
                        t_ld[(wn, q)] = sS.n

                load_half(0)
                load_half(1)
                load_half(2)
                pg_free = [0, 0, 0, 0]
                pu_free = [0, 0, 0, 0]
                hq_free = [0, 0]
                sg_guard = 0
                hdbg_free = [0]
                for fi, (f0, fw) in enumerate(FWINS):
                    sel = 128 if fi == len(FWINS) - 1 else 0
                    for tt in range(8):
                        par = tt % 4
                        PE.wait_ge(sS.h, t_ld[("u", 2 * fi + 1)])
                        PE.wait_ge(sC.h, t_xqf)
                        PE.wait_ge(sAC.h, pg_free[par])
                        PE.wait_ge(sDV.h, pu_free[par])
                        for wn, pp in (("g", ps_g), ("u", ps_u)):
                            mm = None
                            for dc in range(DCH):
                                slot = wring[wn][(2 * fi + dc // 16) % 3]
                                mm = PE.matmul(
                                    pp[par][:, 0:fw],
                                    xq_fm[:, dc, tt * 128:(tt + 1) * 128],
                                    slot[:, dc % 16, 0:fw],
                                    start=(dc == 0), stop=(dc == DCH - 1))
                            sPE.inc(mm)
                        t_mm = sPE.n
                        A.wait_ge(sPE.h, t_mm)
                        A.wait_ge(sDV.h, sg_guard)
                        t_si = sAC.inc(A.activation(sg_t[:, 0:fw],
                                                    ps_g[par][:, 0:fw],
                                                    AF.Silu))
                        pg_free[par] = t_si
                        V.wait_ge(sAC.h, t_si)
                        if DEBUG_TAPS:
                            V.wait_ge(sG.h, hdbg_free[0])
                        t_h = sDV.inc(V.tensor_mul(h_t[:, 0:fw],
                                                   sg_t[:, 0:fw],
                                                   ps_u[par][:, 0:fw]))
                        pu_free[par] = t_h
                        sg_guard = t_h
                        if DEBUG_TAPS:
                            G.wait_ge(sDV.h, t_h)
                            hdbg_free[0] = sG.inc(G.dma_start(
                                out=h_dbg[tt * 128:(tt + 1) * 128,
                                          f0:f0 + fw],
                                in_=h_t[:, 0:fw]), 16)
                        for e in (G, A):
                            e.wait_ge(sDV.h, t_h)
                        hb = tt % 2
                        V.wait_ge(sG.h, hq_free[hb])
                        t_q = emit_quant(wsq, h_t[:, 0:fw], hqb[hb][:, 0:fw],
                                         fw, sel)
                        G.wait_ge(sDV.h, t_q)
                        hq_free[hb] = sG.inc(G.dma_start(
                            out=hq_tm[tt * 128:(tt + 1) * 128, f0:f0 + fw],
                            in_=hqb[hb][:, 0:fw]), 16)
                    t_use[("g", 2 * fi)] = sPE.n
                    t_use[("g", 2 * fi + 1)] = sPE.n
                    t_use[("u", 2 * fi)] = sPE.n
                    t_use[("u", 2 * fi + 1)] = sPE.n
                    if 2 * fi + 3 < 2 * len(FWINS):
                        load_half(2 * fi + 3)
                    if 2 * fi + 4 < 2 * len(FWINS):
                        load_half(2 * fi + 4)
                if DEBUG_TAPS:
                    G.wait_ge(sPE.h, sPE.n)
                    G.wait_ge(sG.h, sG.inc(G.dma_start(out=xqf_dbg2,
                                                       in_=xq_fm[:]), 16))
            barrier()

            with ExitStack() as es:
                hq_res = sb(es, "hq_res", [128, FCH, 512], BF16)
                NWSL = 8
                wd_rg = [sb(es, f"wd{i}", [128, 512], BF16) for i in range(NWSL)]
                scale_sb = sb(es, "scale_sb", [128, D], F32)
                B_sb = sb(es, "B_sb", [64, D], BF16)
                x_t = [sb(es, f"x_t{i}", [128, 512], F32) for i in range(2)]
                s_t = [sb(es, f"s_t{i}", [128, 512], BF16) for i in range(2)]
                g0 = sb(es, "g0", [128, 512], F32)
                g1 = sb(es, "g1", [128, 512], F32)
                g_bf = sb(es, "g_bf", [128, 512], BF16)
                m_t = sb(es, "m_t", [128, 512], F32)
                o_t = [sb(es, f"o_t{i}", [128, 512], F32) for i in range(2)]
                ps_d = [psum(es, f"psd{i}", [128, 512], F32) for i in range(8)]

                t_sc2 = sS.inc(SY.dma_start(out=scale_sb[:], in_=scale_rep), 16)
                t_B = sG.inc(G.dma_start(out=B_sb[:], in_=lrB), 16)

                wd_tick = {}
                wd_read = {}
                NTEN = 16 * FCH

                def load_wd(m):
                    dw_g, f = divmod(m, FCH)
                    dw = dw_g % 8
                    if m - NWSL in wd_read:
                        SY.wait_ge(sPE.h, wd_read[m - NWSL])
                    sS.inc(SY.dma_start(
                        out=wd_rg[m % NWSL][:],
                        in_=wdqT_g[dw * FF + f * 128:dw * FF + (f + 1) * 128,
                                   :]), 16)
                    wd_tick[m] = sS.n

                SY.wait_ge(sG.h, sG.n)
                SY.wait_ge(sCCD.h, 1)
                for m in range(NWSL):
                    load_wd(m)
                pd_free = [0] * 8
                x_free = [0, 0]
                o_free = [0, 0]
                for half in range(2):
                    A.wait_ge(sPE.h, sPE.n)
                    for f in range(FCH):
                        sC.inc(A.dma_start(
                            out=hq_res[:, f, :],
                            in_=hq_tm[half * 512:(half + 1) * 512,
                                      f * 128:(f + 1) * 128],
                            transpose=True), 16)
                    t_hq = sC.n
                    for dw in range(8):
                        m0 = (half * 8 + dw) * FCH
                        pbase = (dw % 2) * 4
                        for f in range(FCH):
                            m = m0 + f
                            PE.wait_ge(sS.h, wd_tick[m])
                            if f == 0:
                                PE.wait_ge(sC.h, t_hq)
                                for sl4 in range(4):
                                    PE.wait_ge(sDV.h, pd_free[pbase + sl4])
                            mm = None
                            for tt in range(4):
                                mm = PE.matmul(
                                    ps_d[pbase + tt][:, :],
                                    hq_res[:, f, tt * 128:(tt + 1) * 128],
                                    wd_rg[m % NWSL][:],
                                    start=(f == 0), stop=(f == FCH - 1))
                            wd_read[m] = sPE.inc(mm)
                            if m + NWSL < NTEN:
                                load_wd(m + NWSL)
                        t_dw = sPE.n
                        for tt in range(4):
                            sl = pbase + tt
                            rows = half * 512 + tt * 128
                            V.wait_ge(sPE.h, t_dw)
                            V.wait_ge(sS.h, t_sc2)
                            t_g0 = sDV.inc(V.tensor_mul(
                                g0[:], ps_d[sl][:, :],
                                scale_sb[:, dw * 512:(dw + 1) * 512]))
                            PE.wait_ge(sDV.h, t_g0)
                            PE.wait_ge(sG.h, t_B)
                            t_t2 = sPE.inc(PE.matmul(
                                ps_d[sl][:, :],
                                t1_bf[0:64, rows:rows + 128],
                                B_sb[0:64, dw * 512:(dw + 1) * 512],
                                start=True, stop=True))
                            V.wait_ge(sPE.h, t_t2)
                            t_g1 = sDV.inc(V.tensor_add(g1[:], g0[:],
                                                        ps_d[sl][:, :]))
                            pd_free[sl] = t_g1
                            A.wait_ge(sDV.h, t_g1)
                            t_gb = sAC.inc(A.activation(g_bf[:], g1[:],
                                                        AF.Silu))
                            bx = tt % 2
                            SY.wait_ge(sDV.h, x_free[bx])
                            t_x = sS.inc(SY.dma_start(
                                out=x_t[bx][:],
                                in_=x_ext[1024 + rows:1024 + rows + 128,
                                          dw * 512:(dw + 1) * 512]), 16)
                            t_s = sS.inc(SY.dma_start(
                                out=s_t[bx][:],
                                in_=s_tm[rows:rows + 128,
                                         dw * 512:(dw + 1) * 512]), 16)
                            V.wait_ge(sAC.h, t_gb)
                            V.wait_ge(sS.h, t_s)
                            V.tensor_mul(m_t[:], s_t[bx][:], g_bf[:])
                            bo = tt % 2
                            V.wait_ge(sOUT.h, o_free[bo])
                            t_o = sDV.inc(V.tensor_add(o_t[bo][:], m_t[:],
                                                       x_t[bx][:]))
                            x_free[bx] = t_o
                            SY.wait_ge(sDV.h, t_o)
                            o_free[bo] = sOUT.inc(SY.dma_start(
                                out=out_d[rows:rows + 128,
                                          dw * 512:(dw + 1) * 512],
                                in_=o_t[bo][:]), 16)
            SY.wait_ge(sOUT.h, 16 * 64)

    return nc


def stage_inputs(inputs):
    f32 = np.float32
    x = np.asarray(inputs["L0_attn_x"], dtype=f32)[0]
    up = np.asarray(inputs["L0_ffn_up_reorder_index"]).astype(np.int64)
    dn = np.asarray(inputs["L0_ffn_down_reorder_index"]).astype(np.int64)
    wq = np.asarray(inputs["L0_attn_Wq"], dtype=f32)
    wk = np.asarray(inputs["L0_attn_Wk"], dtype=f32)
    wv = np.asarray(inputs["L0_attn_Wv"], dtype=f32)
    wo = np.asarray(inputs["L0_attn_Wo"], dtype=f32)
    wg = np.asarray(inputs["L0_ffn_w_gate"], dtype=f32)
    wu = np.asarray(inputs["L0_ffn_w_up"], dtype=f32)
    wd = np.asarray(inputs["L0_ffn_w_down"], dtype=f32)
    mscale = np.asarray(inputs["L0_ffn_mlp_output_scale"], dtype=f32)
    lA = np.asarray(inputs["L0_ffn_lr_A"], dtype=f32)
    lB = np.asarray(inputs["L0_ffn_lr_B"], dtype=f32)

    wqT = np.ascontiguousarray(wq.T)
    wkT = np.ascontiguousarray(wk.T)
    wvT = np.ascontiguousarray(wv.T)
    woT = np.ascontiguousarray(wo.T)
    wg_pad = np.zeros((8 * 1408, D), dtype=f32)
    wg_pad[:FF] = wg[dn][:, up]
    wu_pad = np.zeros((8 * 1408, D), dtype=f32)
    wu_pad[:FF] = wu[dn][:, up]
    wd_s = np.ascontiguousarray(wd[:, dn])
    scale_rep = np.ascontiguousarray(np.broadcast_to(mscale, (128, D)))
    ident = np.eye(128, dtype=BF)
    lA_c = np.ascontiguousarray(lA)
    lB_c = np.ascontiguousarray(lB)

    inv = (1.0 / 10000.0 ** (np.arange(0, HD, 2, dtype=np.float64) / HD))

    in_maps = []
    for c in range(NCORES):
        if c == 0:
            x_ext = np.concatenate(
                [np.zeros((TOK, D), f32), x[0:TOK]], axis=0)
        else:
            x_ext = x[TOK * (c - 1):TOK * (c + 1)]
        xp_own = np.ascontiguousarray(x[TOK * c:TOK * (c + 1)][:, up])
        pos = np.arange(TOK * (c - 1), TOK * (c + 1), dtype=np.float64)
        ang = pos[:, None] * inv[None, :]
        emb = np.concatenate([ang, ang], axis=-1)
        cos_t = np.cos(emb).astype(f32)
        sin_t = np.sin(emb).astype(f32)
        qt_i = np.arange(8)[:, None, None]
        a_i = np.arange(128)[None, :, None]
        kk_i = np.arange(KSPAN)[None, None, :]
        qg = TOK * c + qt_i * 128 + a_i
        kg = TOK * (c - 1) + qt_i * 128 + kk_i
        allowed = (kg >= 0) & (kg <= qg) & (kg > qg - 1024)
        masks = np.where(allowed, np.float32(0.0),
                         np.float32(-1.0e30)).astype(BF)
        in_maps.append({
            "x_ext": np.ascontiguousarray(x_ext),
            "xp_own": xp_own,
            "wqT": wqT, "wkT": wkT, "wvT": wvT, "woT": woT,
            "wg_sl": wg_pad[1408 * c:1408 * (c + 1)],
            "wu_sl": wu_pad[1408 * c:1408 * (c + 1)],
            "wd_sl": wd_s[512 * c:512 * (c + 1)],
            "lrA": lA_c, "lrB": lB_c,
            "scale_rep": scale_rep,
            "cos_t": cos_t, "sin_t": sin_t,
            "masks": np.ascontiguousarray(masks),
            "ident": ident,
        })
    return in_maps


def kernel(**inputs):
    in_maps = stage_inputs(inputs)
    nc = build_nc()
    res = run_bass_kernel_spmd(nc, in_maps, core_ids=list(range(NCORES)))
    out = np.concatenate([res.results[i]["out"] for i in range(NCORES)],
                         axis=0)
    return out[None].astype(np.float32)


if __name__ == "__main__":
    nc = build_nc()
    print("built ok")
